# version 08
# baseline (speedup 1.0000x reference)
import math
import numpy as np
from contextlib import ExitStack

import concourse.bass as bass
import concourse.mybir as mybir
from concourse.bass_utils import run_bass_kernel_spmd
import ml_dtypes

F32 = mybir.dt.float32
BF16 = mybir.dt.bfloat16
AF = mybir.ActivationFunctionType
ALU = mybir.AluOpType
AX = mybir.AxisListType

B, S, D = 2, 4096, 4096
NCORES = 8
SLICES = 4
TOK = 1024
NMB = 32
KVMB = 4
HEADS = 32
KVH = 4
NKC = 32
SCALE = 1.0 / math.sqrt(128.0)
RANKS = (8, 16, 32, 64)
EPS = 1e-5
QCHUNK = 4
NCH = NMB // QCHUNK

bf16 = ml_dtypes.bfloat16

ADR = {}
_a = 0
def _put(name, nbytes, align=32):
    global _a
    _a = (_a + align - 1) // align * align
    ADR[name] = _a
    _a += nbytes
_put("ident_bf", 256); _put("ident32", 512); _put("ones128", 32)
_put("onesrow32", 512); _put("maskv", 128)
_put("bq", 128); _put("bo", 128); _put("bfb", 128); _put("bk", 64); _put("bv", 64)
_put("gb", 32)
_put("negc", 256)
_put("qb2", 256)
_put("mk2", 64)
_put("st8", 256)
_put("gwt_sb", 256)
_put("at_sb", 4096)
assert _a <= 7168, _a
ADR["B"] = 7168
ADR["C"] = ADR["B"] + 32768
ADR["D"] = ADR["C"] + 33792
ADR["E"] = ADR["D"] + 65536
ADR["F"] = ADR["E"] + 65536
assert ADR["F"] + 8063 <= 212863
ADR["wbuf32"] = ADR["B"]
ADR["wbufbf"] = ADR["B"] + 8192
ADR["y_sb"] = ADR["B"] + 12288
ADR["y2bf"] = ADR["B"] + 20480; ADR["ybf"] = ADR["B"] + 22528
ADR["bt"] = ADR["B"] + 24576
ADR["tscrow"] = ADR["F"]
ADR["m_bc"] = ADR["C"]; ADR["r_bc"] = ADR["C"] + 4096
ADR["rowA"] = ADR["C"] + 8192; ADR["rowB"] = ADR["C"] + 12288
ADR["inbuf"] = ADR["C"] + 16384
ADR["tsc"] = ADR["C"] + 24576
ADR["xfmb"] = ADR["D"]
ADR["xsq"] = ADR["D"] + 16384
ADR["sqdum"] = ADR["D"] + 20480
ADR["tsc1"] = ADR["D"] + 28672
ADR["m1b"] = ADR["D"] + 36864; ADR["r1b"] = ADR["D"] + 40960
ADR["row1A"] = ADR["D"] + 45056; ADR["row1B"] = ADR["D"] + 49152
ADR["trow1"] = ADR["D"] + 53248
ADR["umask"] = ADR["D"]; ADR["masksb"] = ADR["D"] + 2048
ADR["gsb"] = ADR["D"] + 4096; ADR["gtm"] = ADR["D"] + 8192
ADR["ohem"] = ADR["D"] + 8384; ADR["ohtm"] = ADR["D"] + 10432
ADR["mxg"] = ADR["D"] + 10496
ADR["btstage"] = ADR["D"] + 16384
ADR["ostage"] = ADR["D"]

ADR["wbufF32"] = ADR["F"]; ADR["wbufFbf"] = ADR["F"] + 4096
ADR["qsq"] = ADR["F"] + 6144
ADR["probs"] = ADR["F"]
ADR["nrm"] = ADR["F"] + 6144
_SBUF_BASE = 17024
for _k in list(ADR):
    ADR[_k] += _SBUF_BASE
assert ADR["qsq"] + 1024 <= 229247 - 129, ADR["qsq"]


class Mock:
    class MInstr:
        def then_inc(self, sem, k):
            return self
    _MI = MInstr()
    def __getattr__(self, name):
        def f(*a, **k):
            return Mock._MI
        return f


class Builder:
    def __init__(self, nc, phase_limit=99, taps=()):
        self.nc = nc
        self.phase_limit = phase_limit
        self.taps = set(taps)
        self.cnt = {}
        self.sems = {}
        self.t = {}
        self.s = {}
        self.extra_outs = []

    def declare(self, es):
        nc, t, s = self.nc, self.t, self.s
        def din(name, shape, dt=F32):
            t[name] = nc.dram_tensor(name, shape, dt, kind="ExternalInput").ap()
        din("x_fm", [D, S])
        din("wq_l", [NMB, 128, NMB, 128]); din("wo_l", [NMB, 128, NMB, 128])
        din("wf_l", [NMB, 128, NMB, 128])
        din("wk_l", [KVMB, 128, NMB, 128]); din("wv_l", [KVMB, 128, NMB, 128])
        din("at_l", [128, NMB * 64]); din("bt_l", [64, D]); din("gwt_l", [128, NMB * 4])
        din("bq_r", [128, NMB]); din("bo_r", [128, NMB]); din("bf_r", [128, NMB])
        din("bk_r", [128, KVMB]); din("bv_r", [128, KVMB]); din("gb_r", [4, 1])
        din("ident_bf", [128, 128], BF16); din("ident32", [128, 128])
        din("ones128", [128, 1], BF16); din("onesrow32", [1, 128])
        din("maskv", [4, 64], BF16)
        t["out"] = nc.dram_tensor("out", [TOK, D], F32, kind="ExternalOutput").ap()
        t["ysc"] = nc.dram_tensor("ysc", [NMB, 128, TOK], F32, kind="Internal").ap()
        t["x1sc"] = nc.dram_tensor("x1sc", [NMB, 128, TOK], F32, kind="Internal").ap()
        t["rowsc"] = nc.dram_tensor("rowsc", [2, 8, 128], F32, kind="Internal").ap()

        def at(name, shape, dt, key=None):
            h = nc.alloc_sbuf_tensor_at(name, shape, dt, offset=ADR[key or name])
            s[name] = h.ap()
        at("ident_bf", [128, 128], BF16); at("ident32", [128, 128], F32)
        at("ones128", [128, 1], BF16); at("onesrow32", [1, 128], F32)
        at("maskv", [4, 64], BF16)
        at("bq", [128, NMB], F32); at("bo", [128, NMB], F32); at("bfb", [128, NMB], F32)
        at("bk", [128, KVMB], F32); at("bv", [128, KVMB], F32); at("gb", [4, 1], F32)
        at("negc", [128, 2 * HEADS], F32)
        at("qb2", [1, 2 * HEADS], F32); at("mk2", [1, 2 * KVH], F32)
        at("st8", [128, 8, 8], F32)
        at("K", [128, KVMB, S], BF16, "B")
        at("V", [128, KVMB, S], BF16, "C")
        at("Vaug", [128, KVH, NKC, 132], BF16, "C")
        at("QA", [128, HEADS, TOK], BF16, "D")
        at("XN", [128, NMB, TOK], BF16, "E")
        at("xfmb", [128, 4, TOK], F32); at("xsq", [128, 2, TOK], BF16)
        at("sqdum", [128, D], BF16); at("tsc1", [128, 2, TOK], F32)
        at("m1b", [128, TOK], F32); at("r1b", [128, TOK], F32)
        at("row1A", [1, TOK], F32); at("row1B", [1, TOK], F32)
        at("trow1", [1, TOK], F32)
        at("wbufF32", [128, 2, QCHUNK, 128], F32); at("wbufFbf", [128, 2, QCHUNK, 128], BF16)
        at("qsq", [128, 512], BF16)
        at("probs", [128, 3, TOK], BF16)
        at("nrm", [128, 1], F32)
        at("wbuf32", [128, 2, 8, 128], F32); at("wbufbf", [128, 2, 8, 128], BF16)
        at("y_sb", [128, 2, TOK], F32); at("y2bf", [128, TOK], BF16); at("ybf", [128, TOK], BF16)
        at("bt", [64, D], BF16); at("tscrow", [1, TOK], F32)
        at("m_bc", [128, TOK], F32); at("r_bc", [128, TOK], F32)
        at("rowA", [1, TOK], F32); at("rowB", [1, TOK], F32)
        at("inbuf", [128, 2, TOK], F32); at("tsc", [128, 2, TOK], F32)
        at("umask", [64, TOK], BF16); at("masksb", [64, TOK], BF16)
        at("gsb", [4, TOK], F32); at("gtm", [128, 8, 4], F32)
        at("ohem", [4, 8, 128], BF16); at("ohtm", [128, 8, 4], BF16)
        at("mxg", [128, 8, 1], F32)
        at("ostage", [128, 2, 8, 128], F32); at("btstage", [64, D], F32)
        
        at("gwt_sb", [128, NMB, 4], BF16); at("at_sb", [128, NMB, 64], BF16)

        self.ps = nc.alloc_psum_tensor("psmaster", [128, 8, 512], F32).ap()

        for name in ["sX0", "sX1", "sF0", "sF1", "sW0", "sW1", "sW2", "sW3", "sLg",
                     "sSa", "sSb", "sSg", "sCC", "sPE", "sSC",
                     "sPV", "sEX", "sACT", "sDVE", "sVV", "sPOOL", "sBAR", "sPE2"]:
            self.sems[name] = es.enter_context(nc.semaphore(name))
            self.cnt[name] = 0

        if "ph1" in self.taps:
            self.tap("tap_xn1", [128, NMB, TOK], BF16)
            self.tap("tap_K", [128, KVMB, S], BF16)
            self.tap("tap_V", [128, KVMB, S], BF16)
            self.tap("tap_mk2", [1, 2 * KVH], F32)
        if "ph2" in self.taps:
            self.tap("tap_QA", [128, HEADS, TOK], BF16)
            self.tap("tap_negc", [128, 2 * HEADS], F32)
        if "ph25" in self.taps:
            self.tap("tap_vaug", [128, KVH, NKC, 132], BF16)
        if "ph3" in self.taps:
            self.tap("tap_att", [128, HEADS, TOK], BF16)
        if "ph4a" in self.taps:
            self.tap("tap_y", [NMB, 128, TOK], F32)
        if "ph4b" in self.taps:
            self.tap("tap_x1", [NMB, 128, TOK], F32)
            self.tap("tap_xn2", [128, NMB, TOK], BF16)
            self.tap("tap_y", [NMB, 128, TOK], F32)
        if "ph5a" in self.taps:
            self.tap("tap_um", [64, TOK], BF16)
            self.tap("tap_oh", [4, 8, 128], BF16)

    def tap(self, name, shape, dt):
        self.t[name] = self.nc.dram_tensor(name, shape, dt, kind="ExternalOutput").ap()
        self.extra_outs.append(name)

    def inc(self, instr, sem, k=1):
        instr.then_inc(self.sems[sem], k)
        self.cnt[sem] += k
        return self.cnt[sem]

    def w(self, eng, sem, val):
        eng.wait_ge(self.sems[sem], val)

    ENGINE_SEMS = {"sync": ("sX0", "sX1", "sF0", "sF1", "sW0", "sW1", "sW2", "sW3", "sLg"),
                   "tensor": ("sPE", "sSC", "sPV", "sPE2"),
                   "vector": ("sDVE", "sVV"), "scalar": ("sACT", "sEX"),
                   "gpsimd": ("sPOOL", "sSa", "sSb", "sSg", "sCC")}

    @staticmethod
    def xsem(i):
        return "sX0" if i % 2 == 0 else "sX1"

    @staticmethod
    def fsem(i):
        return "sF0" if i % 2 == 0 else "sF1"

    @staticmethod
    def wsem(i):
        return ("sW0", "sW1", "sW2", "sW3")[i % 4]

    @staticmethod
    def ssem(i):
        return "sSa" if i % 2 == 0 else "sSb"

    def bar(self, E):
        for p in ("sync", "tensor", "vector", "scalar", "gpsimd"):
            for sem in self.ENGINE_SEMS[p]:
                self.w(E[p], sem, self.cnt[sem])
            self.inc(E[p].drain(), "sBAR", 1)
        tgt = self.cnt["sBAR"]
        for p in ("sync", "tensor", "vector", "scalar", "gpsimd"):
            self.w(E[p], "sBAR", tgt)

    def emit(self):
        nc = self.nc
        es = ExitStack()
        with es:
            self.declare(es)
            block = es.enter_context(nc.Block())
            passes = ["sync", "tensor", "vector", "scalar", "gpsimd"]

            def run_pass(which, eng):
                for k in self.cnt:
                    self.cnt[k] = 0
                E = {p: (eng if p == which else Mock()) for p in passes}
                self.program(E)

            block.sync(lambda e: run_pass("sync", e))
            block.tensor(lambda e: run_pass("tensor", e))
            block.vector(lambda e: run_pass("vector", e))
            block.scalar(lambda e: run_pass("scalar", e))
            block.gpsimd(lambda e: run_pass("gpsimd", e))

    def program(self, E):
        self.ph0(E)
        self.bar(E)
        if self.phase_limit < 1:
            self.finish(E); return
        for ss in (1, 2, 3, 0):
            self.ph1_slice(E, ss)
            self.bar(E)
        self.ph1_knorm(E)
        self.bar(E)
        if "ph1" in self.taps:
            self.tapc(E, "tap_xn1", self.s["XN"])
            self.tapc(E, "tap_K", self.s["K"])
            self.tapc(E, "tap_V", self.s["V"])
            self.tapc(E, "tap_mk2", self.s["mk2"])
        if self.phase_limit < 2:
            self.finish(E); return
        self.ph2(E)
        self.bar(E)
        if "ph2" in self.taps:
            self.tapc(E, "tap_QA", self.s["QA"])
            self.tapc(E, "tap_negc", self.s["negc"])
        if self.phase_limit < 3:
            self.finish(E); return
        self.ph25(E)
        self.bar(E)
        if "ph25" in self.taps:
            self.tapc(E, "tap_vaug", self.s["Vaug"])
        if self.phase_limit < 4:
            self.finish(E); return
        self.ph3(E)
        self.bar(E)
        self.ph35(E)
        self.bar(E)
        if "ph3" in self.taps:
            self.tapc(E, "tap_att", self.s["QA"])
        if self.phase_limit < 5:
            self.finish(E); return
        self.ph4a(E)
        self.bar(E)
        if "ph4a" in self.taps:
            self.tapc(E, "tap_y", self.t["ysc"])
        if self.phase_limit < 6:
            self.finish(E); return
        self.ph4b(E)
        self.bar(E)
        if "ph4b" in self.taps:
            self.tapc(E, "tap_x1", self.t["x1sc"])
            self.tapc(E, "tap_xn2", self.s["XN"])
            self.tapc(E, "tap_y", self.t["ysc"])
        if self.phase_limit < 7:
            self.finish(E); return
        self.ph5a(E)
        self.bar(E)
        if "ph5a" in self.taps:
            self.tapc(E, "tap_um", self.s["umask"])
            self.tapc(E, "tap_oh", self.s["ohem"])
        if self.phase_limit < 8:
            self.finish(E); return
        self.ph5b(E)
        self.bar(E)
        self.ph5c(E)
        self.finish(E)

    def tapc(self, E, name, src_ap):
        n = self.inc(E["gpsimd"].dma_start(out=self.t[name], in_=src_ap), "sSg", 16)
        self.w(E["gpsimd"], "sSg", n)

    def finish(self, E):
        for sem in ("sSa", "sSb", "sSg"):
            self.w(E["gpsimd"], sem, self.cnt[sem])

    def ph0(self, E):
        t, s = self.t, self.s
        q, g = E["sync"], E["gpsimd"]
        for dst, src in [("ident_bf", "ident_bf"), ("ident32", "ident32"),
                         ("ones128", "ones128"), ("onesrow32", "onesrow32"),
                         ("maskv", "maskv"), ("gb", "gb_r"), ("bq", "bq_r"),
                         ("bo", "bo_r"), ("bfb", "bf_r"), ("bk", "bk_r"),
                         ("bv", "bv_r")]:
            self.inc(q.dma_start(out=s[dst], in_=t[src]), "sLg", 16)
        gstage = s["inbuf"].rearrange("p a f -> p (a f)")[:, 0:NMB * 4]
        astage = s["tsc"].rearrange("p a f -> p (a f)")[:, 0:NMB * 64]
        self.inc(q.dma_start(out=gstage, in_=t["gwt_l"]), "sLg", 16)
        self.inc(q.dma_start(out=astage, in_=t["at_l"]), "sLg", 16)
        self.w(g, "sLg", self.cnt["sLg"])
        self.inc(g.tensor_copy(s["gwt_sb"].rearrange("p c f -> p (c f)"), gstage), "sPOOL")
        self.inc(g.tensor_copy(s["at_sb"].rearrange("p c f -> p (c f)"), astage), "sPOOL")

    def ph1_slice(self, E, ss):
        t, s, ps = self.t, self.s, self.ps
        q, pe, v, sc, g = E["sync"], E["tensor"], E["vector"], E["scalar"], E["gpsimd"]
        base = dict(self.cnt)
        tok0 = ss * TOK

        fm, cmarks, sqm = [], [], []
        for mb in range(NMB):
            if mb >= 4:
                self.w(q, "sVV", cmarks[mb - 4])
            fm.append(self.inc(q.dma_start(out=s["xfmb"][:, mb % 4, :],
                                 in_=t["x_fm"][mb * 128:(mb + 1) * 128, tok0:tok0 + TOK]),
                               self.wsem(mb), 16))
            self.w(v, self.wsem(mb), fm[mb])
            cmarks.append(self.inc(v.tensor_copy(s["XN"][:, mb, :], s["xfmb"][:, mb % 4, :]), "sVV"))
            self.w(sc, "sVV", cmarks[mb])
            if mb >= 2:
                self.w(sc, "sPE", base["sPE"] + mb - 1)
            sqm.append(self.inc(sc.activation(s["xsq"][:, mb % 2, :], s["XN"][:, mb, :],
                                              AF.Square), "sACT"))
            self.w(pe, "sVV", cmarks[mb])
            self.w(pe, "sACT", sqm[mb])
            for half in range(2):
                pe.matmul(ps[0:1, 4 + half, :], s["ones128"],
                          s["XN"][:, mb, half * 512:(half + 1) * 512],
                          start=(mb == 0), stop=(mb == NMB - 1), skip_group_check=True)
            for half in range(2):
                ins = pe.matmul(ps[0:1, 6 + half, :], s["ones128"],
                                s["xsq"][:, mb % 2, half * 512:(half + 1) * 512],
                                start=(mb == 0), stop=(mb == NMB - 1), skip_group_check=True)
            self.inc(ins, "sPE")
        self.stats_to_bcast(E, rA="row1A", rB="row1B", trow="trow1",
                            mbc="m1b", rbc="r1b")
        nbc = self.cnt["sVV"]
        base2 = dict(self.cnt)
        apm = []
        for mb in range(NMB):
            if mb == 0:
                self.w(v, "sVV", nbc)
            if mb >= 2:
                self.w(v, "sPOOL", apm[mb - 2])
            nv = self.inc(v.tensor_tensor(s["tsc1"][:, mb % 2, :], s["XN"][:, mb, :],
                                          s["m1b"], ALU.add), "sDVE")
            self.w(g, "sDVE", nv)
            if mb == 0:
                self.w(g, "sVV", nbc)
            apm.append(self.inc(g.tensor_tensor(s["XN"][:, mb, :], s["tsc1"][:, mb % 2, :],
                                                s["r1b"], ALU.mult), "sPOOL"))
        npool_apply = self.cnt["sPOOL"]

        base3 = dict(self.cnt)
        wm, cm = [], []
        for mbkv in range(2 * KVMB):
            wsrc = t["wk_l"] if mbkv < KVMB else t["wv_l"]
            mbl = mbkv % KVMB
            pbank = (mbkv % 2) * 2
            if mbkv >= 2:
                self.w(pe, "sACT", base3["sACT"] + mbkv - 1)
            for ch in range(NCH):
                it = mbkv * NCH + ch
                if it >= 2:
                    self.w(q, "sVV", cm[it - 2])
                else:
                    self.w(q, "sVV", cmarks[NMB - 1])
                wm.append(self.inc(q.dma_start(out=s["wbufF32"][:, it % 2, :, :],
                                     in_=wsrc[mbl, :, ch * QCHUNK:(ch + 1) * QCHUNK, :]),
                                   self.wsem(it), 16))
                self.w(v, self.wsem(it), wm[it])
                if it >= 2:
                    self.w(v, "sPE2", base3["sPE2"] + 4 * (it - 1))
                cm.append(self.inc(v.tensor_copy(s["wbufFbf"][:, it % 2, :, :],
                                                 s["wbufF32"][:, it % 2, :, :]), "sVV"))
                if it == 0:
                    self.w(pe, "sPOOL", npool_apply)
                self.w(pe, "sVV", cm[it])
                for kq in range(QCHUNK):
                    kb = ch * QCHUNK + kq
                    for half in range(2):
                        ins = pe.matmul(ps[:, pbank + half, :],
                                        s["wbufFbf"][:, it % 2, kq, :],
                                        s["XN"][:, kb, half * 512:(half + 1) * 512],
                                        start=(kb == 0), stop=(kb == NMB - 1),
                                        skip_group_check=True)
                    self.inc(ins, "sPE2")
            dst = s["K"] if mbkv < KVMB else s["V"]
            bias = s["bk"] if mbkv < KVMB else s["bv"]
            self.w(sc, "sPE2", base3["sPE2"] + (mbkv + 1) * NCH * 4)
            self.inc(sc.activation(dst[:, mbkv % KVMB, tok0:tok0 + TOK],
                                   ps[:, pbank:pbank + 2, :].rearrange("p a b -> p (a b)"),
                                   AF.Identity, bias=bias[:, mbkv % KVMB:mbkv % KVMB + 1],
                                   scale=1.0), "sACT")

    def ph1_knorm(self, E):
        s, ps = self.s, self.ps
        pe, v, sc = E["tensor"], E["vector"], E["scalar"]
        base = dict(self.cnt)
        for kv in range(KVH):
            if kv >= 1:
                self.w(sc, "sPE", base["sPE"] + kv * 8)
            na = self.inc(sc.activation(s["sqdum"], s["K"][:, kv, :], AF.Square), "sACT")
            self.w(pe, "sACT", na)
            for part in range(8):
                gidx = kv * 8 + part
                if gidx >= 2:
                    self.w(pe, "sDVE", base["sDVE"] + gidx - 1)
                npe = self.inc(pe.matmul(ps[0:1, part % 2 + 4, :], s["ones128"],
                                         s["sqdum"][:, part * 512:(part + 1) * 512],
                                         start=True, stop=True, skip_group_check=True), "sPE")
                self.w(v, "sPE", npe)
                if part == 0 and kv >= 1:
                    self.w(v, "sVV", base["sVV"] + kv)
                self.inc(v.tensor_reduce(s["st8"][0:1, part, 0:1], ps[0:1, part % 2 + 4, :],
                                         AX.X, ALU.max), "sDVE")
            self.w(v, "sDVE", base["sDVE"] + (kv + 1) * 8)
            nv = self.inc(v.tensor_reduce(s["mk2"][0:1, kv:kv + 1],
                                          s["st8"][0:1, 0:8, 0:1].rearrange("p c o -> p (c o)"),
                                          AX.X, ALU.max), "sVV")
        self.w(sc, "sVV", self.cnt["sVV"])
        self.inc(sc.activation(s["mk2"][0:1, KVH:2 * KVH], s["mk2"][0:1, 0:KVH], AF.Sqrt), "sACT")

    def ph2(self, E):
        t, s, ps = self.t, self.s, self.ps
        q, pe, v, sc, g = E["sync"], E["tensor"], E["vector"], E["scalar"], E["gpsimd"]
        base = dict(self.cnt)
        wm = []
        for mb in range(NMB):
            for ch in range(NCH):
                it = mb * NCH + ch
                if it >= 2:
                    self.w(q, "sVV", base["sVV"] + it - 1)
                wm.append(self.inc(q.dma_start(out=s["wbufF32"][:, it % 2, :, :],
                                     in_=t["wq_l"][mb, :, ch * QCHUNK:(ch + 1) * QCHUNK, :]),
                                   self.wsem(it), 16))
        for it in range(NMB * NCH):
            self.w(v, self.wsem(it), wm[it])
            if it >= 2:
                self.w(v, "sPE2", base["sPE2"] + it - 1)
            self.inc(v.tensor_copy(s["wbufFbf"][:, it % 2, :, :], s["wbufF32"][:, it % 2, :, :]), "sVV")
        for mb in range(NMB):
            pbank = (mb % 2) * 2
            if mb >= 2:
                self.w(pe, "sACT", base["sACT"] + mb - 1)
            for ch in range(NCH):
                it = mb * NCH + ch
                self.w(pe, "sVV", base["sVV"] + it + 1)
                for kq in range(QCHUNK):
                    kb = ch * QCHUNK + kq
                    for half in range(2):
                        ins = pe.matmul(ps[:, pbank + half, :],
                                        s["wbufFbf"][:, it % 2, kq, :],
                                        s["XN"][:, kb, half * 512:(half + 1) * 512],
                                        start=(kb == 0), stop=(kb == NMB - 1),
                                        skip_group_check=True)
                    if kq == QCHUNK - 1:
                        self.inc(ins, "sPE2")
            self.w(sc, "sPE2", base["sPE2"] + (mb + 1) * NCH)
            self.inc(sc.activation(s["QA"][:, mb, :],
                                   ps[:, pbank:pbank + 2, :].rearrange("p a b -> p (a b)"),
                                   AF.Identity, bias=s["bq"][:, mb:mb + 1], scale=1.0), "sACT")
        base2 = dict(self.cnt)
        for h in range(HEADS):
            self.w(sc, "sACT", base["sACT"] + h + 1)
            if h >= 1:
                self.w(pe, "sDVE", base2["sDVE"] + 2 * h)
            for half in range(2):
                if not (h == 0 and half == 0):
                    self.w(sc, "sPE", base2["sPE"] + 2 * h + half)
                na = self.inc(sc.activation(s["qsq"], s["QA"][:, h, half * 512:(half + 1) * 512],
                                            AF.Square), "sACT")
                self.w(pe, "sACT", na)
                npe = self.inc(pe.matmul(ps[0:1, 4 + half, :], s["ones128"],
                                         s["qsq"],
                                         start=True, stop=True, skip_group_check=True), "sPE")
                self.w(v, "sPE", npe)
                if h + half > 0:
                    self.w(v, "sVV", base2["sVV"])
                self.inc(v.tensor_reduce(s["qb2"][0:1, 2 * h + half:2 * h + half + 1],
                                         ps[0:1, 4 + half, :], AX.X, ALU.max), "sDVE")
        self.w(sc, "sDVE", base2["sDVE"] + 2 * HEADS)
        na = self.inc(sc.activation(s["st8"][0:1, 0:8, 0:8].rearrange("p a b -> p (a b)"),
                                    s["qb2"], AF.Sqrt), "sACT")
        self.w(v, "sACT", na)
        nv = self.inc(v.tensor_copy(s["qb2"], s["st8"][0:1, 0:8, 0:8].rearrange("p a b -> p (a b)")), "sVV")
        self.w(v, "sVV", nv)
        for gi in range(KVH):
            for half in range(2):
                self.inc(v.tensor_scalar(s["qb2"][0:1, 2 * gi + half::2 * KVH],
                                         s["qb2"][0:1, 2 * gi + half::2 * KVH],
                                         s["mk2"][0:1, KVH + gi:KVH + gi + 1], SCALE,
                                         ALU.mult, ALU.mult), "sVV")
        nv = self.cnt["sVV"]
        self.w(v, "sVV", nv)
        nv = self.inc(v.tensor_scalar(s["qb2"], s["qb2"], -75.0, None, ALU.add), "sVV")
        self.w(v, "sVV", nv)
        nv = self.inc(v.tensor_scalar(s["qb2"], s["qb2"], 0.0, -1.0, ALU.max, ALU.mult), "sVV")
        self.w(pe, "sVV", nv)
        npe = self.inc(pe.matmul(ps[:, 6, 0:2 * HEADS], s["onesrow32"], s["qb2"],
                                 start=True, stop=True, skip_group_check=True), "sPE")
        self.w(v, "sPE", npe)
        self.inc(v.tensor_copy(s["negc"], ps[:, 6, 0:2 * HEADS]), "sVV")

    def ph25(self, E):
        s, ps = self.s, self.ps
        pe, v = E["tensor"], E["vector"]
        base = dict(self.cnt)
        it = 0
        for kv in range(KVH - 1, -1, -1):
            for kc in range(NKC - 1, -1, -1):
                pb = it % 4
                if it >= 4:
                    self.w(pe, "sDVE", base["sDVE"] + 3 * (it - 3))
                npe = self.inc(pe.transpose(ps[:, pb, 0:64].bitcast(BF16),
                                            s["V"][:, kv, kc * 128:(kc + 1) * 128],
                                            s["ident_bf"]), "sPE")
                self.w(v, "sPE", npe)
                self.inc(v.tensor_copy(s["Vaug"][:, kv, kc, 0:128],
                                       ps[:, pb, 0:64].bitcast(BF16)), "sDVE")
                self.inc(v.memset(s["Vaug"][:, kv, kc, 128:129], 1.0), "sDVE")
                self.inc(v.memset(s["Vaug"][:, kv, kc, 129:132], 0.0), "sDVE")
                it += 1

    def ph3(self, E):
        s, ps = self.s, self.ps
        pe, v, sc = E["tensor"], E["vector"], E["scalar"]
        base = dict(self.cnt)
        scv = [ps[:, 0:2, :], ps[:, 2:4, :]]

        def accv(j):
            return ps[:, 4 + j // 2, (j % 2) * 256:(j % 2) * 256 + 129]

        def score_emit(n):
            h, kc = divmod(n, NKC)
            if n >= 2:
                self.w(pe, "sEX", base["sEX"] + n - 1)
            for half in range(2):
                ins = pe.matmul(scv[n % 2][:, half, :],
                                s["K"][:, h % KVH, kc * 128:(kc + 1) * 128],
                                s["QA"][:, h, half * 512:(half + 1) * 512],
                                start=True, stop=True, skip_group_check=True)
            self.inc(ins, "sSC")

        def pv_emit(n):
            h, kc = divmod(n, NKC)
            self.w(pe, "sEX", base["sEX"] + n + 1)
            if kc == 0 and h >= 1:
                self.w(pe, "sDVE", base["sDVE"] + 16 * h)
            for j in range(8):
                ins = pe.matmul(accv(j), s["probs"][:, n % 3, j * 128:(j + 1) * 128],
                                s["Vaug"][:, h % KVH, kc, 0:129],
                                start=(kc == 0 and j % 2 == 0), stop=(kc == NKC - 1),
                                skip_group_check=True)
            self.inc(ins, "sPV")

        NTOT = HEADS * NKC
        score_emit(0)
        score_emit(1)
        for n in range(NTOT):
            pv_emit(n)
            if n + 2 < NTOT:
                score_emit(n + 2)
        for n in range(NTOT):
            h, kc = divmod(n, NKC)
            self.w(sc, "sSC", base["sSC"] + n + 1)
            if n >= 3:
                self.w(sc, "sPV", base["sPV"] + n - 2)
            sc.activation(s["probs"][:, n % 3, 0:512], scv[n % 2][:, 0, :],
                          AF.Exp, bias=s["negc"][:, 2 * h:2 * h + 1], scale=SCALE)
            self.inc(sc.activation(s["probs"][:, n % 3, 512:1024], scv[n % 2][:, 1, :],
                                   AF.Exp, bias=s["negc"][:, 2 * h + 1:2 * h + 2], scale=SCALE), "sEX")
        for h in range(HEADS):
            self.w(v, "sPV", base["sPV"] + (h + 1) * NKC)
            for j in range(8):
                if h + j > 0:
                    self.w(v, "sDVE", base["sDVE"] + 16 * h + 2 * j)
                nv = self.inc(v.reciprocal(s["nrm"], accv(j)[:, 128:129]), "sVV")
                self.w(v, "sVV", nv)
                self.inc(v.tensor_scalar(s["QA"][:, h, j * 128:(j + 1) * 128],
                                         accv(j)[:, 0:128], s["nrm"], None, ALU.mult),
                         "sDVE", 2)

    def ph35(self, E):
        s, ps = self.s, self.ps
        pe, v = E["tensor"], E["vector"]
        base = dict(self.cnt)
        it = 0
        for h in range(HEADS):
            for j in range(8):
                pb = it % 4
                if it >= 4:
                    self.w(pe, "sDVE", base["sDVE"] + it - 3)
                self.inc(pe.transpose(ps[:, pb, 0:64].bitcast(BF16),
                                      s["QA"][:, h, j * 128:(j + 1) * 128],
                                      s["ident_bf"]), "sPE")
                self.w(v, "sPE", base["sPE"] + it + 1)
                self.inc(v.tensor_copy(s["QA"][:, h, j * 128:(j + 1) * 128],
                                       ps[:, pb, 0:64].bitcast(BF16)), "sDVE")
                it += 1

    def wproj(self, E, wname, bias_name, extra_lora=False, rhs_name="XN"):
        t, s, ps = self.t, self.s, self.ps
        q, pe, v, sc, g = E["sync"], E["tensor"], E["vector"], E["scalar"], E["gpsimd"]
        base = dict(self.cnt)
        wsrc = t[wname]
        lora = 1 if extra_lora else 0
        wm, cm, ym, em = [], [], [], []
        for mb in range(NMB):
            pbank = (mb % 2) * 2
            if mb >= 2:
                self.w(pe, "sACT", base["sACT"] + (mb - 1))
                self.w(pe, "sDVE", base["sDVE"] + (mb - 1))
            for ch in range(4):
                it = mb * 4 + ch
                if it >= 2:
                    self.w(q, "sVV", cm[it - 2])
                wm.append(self.inc(q.dma_start(out=s["wbuf32"][:, it % 2, :, :],
                                     in_=wsrc[mb, :, ch * 8:(ch + 1) * 8, :]),
                                   self.wsem(it), 16))
                self.w(v, self.wsem(it), wm[it])
                if it >= 2:
                    self.w(v, "sPE2", em[it - 1])
                cm.append(self.inc(v.tensor_copy(s["wbufbf"][:, it % 2, :, :],
                                                 s["wbuf32"][:, it % 2, :, :]), "sVV"))
                self.w(pe, "sVV", cm[it])
                for kq in range(8):
                    kb = ch * 8 + kq
                    for half in range(2):
                        ne = self.inc(pe.matmul(ps[:, pbank + half, :],
                                        s["wbufbf"][:, it % 2, kq, :],
                                        s[rhs_name][:, kb, half * 512:(half + 1) * 512],
                                        start=(kb == 0),
                                        stop=(kb == NMB - 1 and not extra_lora),
                                        skip_group_check=True), "sPE2")
                em.append(ne)
            if extra_lora:
                for half in range(2):
                    ins = pe.matmul(ps[:, pbank + half, :],
                                    s["bt"][:, mb * 128:(mb + 1) * 128],
                                    s["umask"][:, half * 512:(half + 1) * 512],
                                    start=False, stop=True, skip_group_check=True)
                self.inc(ins, "sPE2")
            self.w(sc, "sPE2", self.cnt["sPE2"])
            if mb >= 2:
                self.w(sc, self.ssem(mb), ym[mb - 2])
                self.w(sc, "sDVE", base["sDVE"] + mb - 1)
                self.w(sc, "sPOOL", pm[mb - 2])
            na = self.inc(sc.activation(s["y_sb"][:, mb % 2, :],
                                        ps[:, pbank:pbank + 2, :].rearrange("p a b -> p (a b)"),
                                        AF.Identity, bias=s[bias_name][:, mb:mb + 1],
                                        scale=1.0), "sACT")
            self.w(v, "sACT", na)
            if mb >= 1:
                self.w(v, "sPE", base["sPE"] + mb)
            self.inc(v.scalar_tensor_tensor(s["y2bf"],
                                            ps[:, pbank:pbank + 2, :].rearrange("p a b -> p (a b)"),
                                            s[bias_name][:, mb:mb + 1], s["y_sb"][:, mb % 2, :],
                                            ALU.add, ALU.mult), "sDVE")
            self.w(g, "sACT", na)
            if mb >= 1:
                self.w(g, "sPE", base["sPE"] + mb)
            if mb == 0:
                pm = []
            pm.append(self.inc(g.tensor_copy(s["ybf"], s["y_sb"][:, mb % 2, :]), "sPOOL"))
            self.w(pe, "sDVE", base["sDVE"] + mb + 1)
            self.w(pe, "sPOOL", pm[mb])
            for half in range(2):
                pe.matmul(ps[0:1, 4 + half, :], s["ones128"],
                          s["ybf"][:, half * 512:(half + 1) * 512],
                          start=(mb == 0), stop=(mb == NMB - 1), skip_group_check=True)
            for half in range(2):
                ins = pe.matmul(ps[0:1, 6 + half, :], s["ones128"],
                                s["y2bf"][:, half * 512:(half + 1) * 512],
                                start=(mb == 0), stop=(mb == NMB - 1), skip_group_check=True)
            self.inc(ins, "sPE")
            self.w(g, "sACT", na)
            ym.append(self.inc(g.dma_start(out=t["ysc"][mb], in_=s["y_sb"][:, mb % 2, :]),
                               self.ssem(mb), 16))

    def stats_to_bcast(self, E, rA="rowA", rB="rowB", trow="tscrow",
                       mbc="m_bc", rbc="r_bc"):
        s, ps = self.s, self.ps
        pe, v, sc = E["tensor"], E["vector"], E["scalar"]
        self.w(v, "sPE", self.cnt["sPE"])
        nv = self.inc(v.tensor_scalar(s[rA][0:1, 0:512], ps[0:1, 4, :], 1.0 / D, None, ALU.mult), "sVV")
        nv = self.inc(v.tensor_scalar(s[rA][0:1, 512:1024], ps[0:1, 5, :], 1.0 / D, None, ALU.mult), "sVV")
        nv = self.inc(v.tensor_scalar(s[rB][0:1, 0:512], ps[0:1, 6, :], 1.0 / D, None, ALU.mult), "sVV")
        nv = self.inc(v.tensor_scalar(s[rB][0:1, 512:1024], ps[0:1, 7, :], 1.0 / D, None, ALU.mult), "sVV")
        self.w(v, "sVV", nv)
        nv = self.inc(v.tensor_tensor(s[trow], s[rA], s[rA], ALU.mult), "sVV")
        self.w(v, "sVV", nv)
        nv = self.inc(v.tensor_tensor(s[rB], s[rB], s[trow], ALU.subtract), "sVV")
        self.w(v, "sVV", nv)
        nv = self.inc(v.tensor_scalar(s[trow], s[rB], EPS, None, ALU.add), "sVV")
        self.w(sc, "sVV", nv)
        na = self.inc(sc.activation(s[rB], s[trow], AF.Sqrt), "sACT")
        self.w(v, "sACT", na)
        nv = self.inc(v.reciprocal(s[trow], s[rB]), "sVV")
        self.w(v, "sVV", nv)
        nv = self.inc(v.tensor_copy(s[rB], s[trow]), "sVV")
        nv = self.inc(v.tensor_scalar(s[rA], s[rA], -1.0, None, ALU.mult), "sVV")
        self.w(pe, "sVV", nv)
        for half in range(2):
            pe.matmul(ps[:, 0 + half, :], s["onesrow32"], s[rA][0:1, half * 512:(half + 1) * 512],
                      start=True, stop=True, skip_group_check=True)
        for half in range(2):
            ins = pe.matmul(ps[:, 2 + half, :], s["onesrow32"], s[rB][0:1, half * 512:(half + 1) * 512],
                            start=True, stop=True, skip_group_check=True)
        npe = self.inc(ins, "sPE")
        self.w(v, "sPE", npe)
        nv = self.inc(v.tensor_copy(s[mbc], ps[:, 0:2, :].rearrange("p a b -> p (a b)")), "sVV")
        self.inc(v.tensor_copy(s[rbc], ps[:, 2:4, :].rearrange("p a b -> p (a b)")), "sVV")

    def ph4a(self, E):
        self.wproj(E, "wo_l", "bo", extra_lora=False, rhs_name="QA")

    def ph4b(self, E):
        t, s, ps = self.t, self.s, self.ps
        q, pe, v, sc, g = E["sync"], E["tensor"], E["vector"], E["scalar"], E["gpsimd"]
        self.stats_to_bcast(E)
        self.bar(E)
        base = dict(self.cnt)
        lm, sm, pms = [], [], []
        for mb in range(NMB):
            if mb >= 2:
                self.w(q, "sDVE", base["sDVE"] + 4 * (mb - 1))
                self.w(q, self.ssem(mb), sm[mb - 2])
            self.inc(q.dma_start(out=s["y_sb"][:, mb % 2, :], in_=t["ysc"][mb]), self.fsem(mb), 16)
            lm.append(self.inc(q.dma_start(out=s["inbuf"][:, mb % 2, :],
                                 in_=t["x_fm"][mb * 128:(mb + 1) * 128, 0:TOK]), self.fsem(mb), 16))
            self.w(v, self.fsem(mb), lm[mb])
            if mb >= 2:
                self.w(v, self.ssem(mb), sm[mb - 2])
                self.w(v, "sPOOL", pms[mb - 2])
            nv = self.inc(v.tensor_tensor(s["tsc"][:, mb % 2, :], s["y_sb"][:, mb % 2, :],
                                          s["m_bc"], ALU.add), "sDVE")
            self.w(g, "sDVE", nv)
            pms.append(self.inc(g.tensor_tensor(s["y_sb"][:, mb % 2, :], s["tsc"][:, mb % 2, :],
                                                s["r_bc"], ALU.mult), "sPOOL"))
            self.w(v, "sPOOL", pms[mb])
            nv = self.inc(v.tensor_tensor(s["tsc"][:, mb % 2, :], s["y_sb"][:, mb % 2, :],
                                          s["inbuf"][:, mb % 2, :], ALU.add), "sDVE")
            self.w(g, "sDVE", nv)
            sm.append(self.inc(g.dma_start(out=t["x1sc"][mb], in_=s["tsc"][:, mb % 2, :]),
                               self.ssem(mb), 16))
            if mb >= 1:
                self.w(v, "sPE", base["sPE"] + mb)
            self.w(v, "sDVE", nv)
            self.inc(v.tensor_copy(s["ybf"], s["tsc"][:, mb % 2, :]), "sDVE")
            nv = self.inc(v.tensor_mul(s["y2bf"], s["tsc"][:, mb % 2, :], s["tsc"][:, mb % 2, :]), "sDVE")
            self.w(pe, "sDVE", nv)
            for half in range(2):
                pe.matmul(ps[0:1, 4 + half, :], s["ones128"],
                          s["ybf"][:, half * 512:(half + 1) * 512],
                          start=(mb == 0), stop=(mb == NMB - 1), skip_group_check=True)
            for half in range(2):
                ins = pe.matmul(ps[0:1, 6 + half, :], s["ones128"],
                                s["y2bf"][:, half * 512:(half + 1) * 512],
                                start=(mb == 0), stop=(mb == NMB - 1), skip_group_check=True)
            self.inc(ins, "sPE")
        self.bar(E)
        self.stats_to_bcast(E)
        self.bar(E)
        base2 = dict(self.cnt)
        lm2 = []
        for mb in range(NMB):
            if mb >= 2:
                self.w(q, "sDVE", base2["sDVE"] + mb - 1)
            lm2.append(self.inc(q.dma_start(out=s["inbuf"][:, mb % 2, :], in_=t["x1sc"][mb]),
                                self.fsem(mb), 16))
            self.w(v, self.fsem(mb), lm2[mb])
            if mb >= 2:
                self.w(v, "sPOOL", base2["sPOOL"] + mb - 1)
            nv = self.inc(v.tensor_tensor(s["tsc"][:, mb % 2, :], s["inbuf"][:, mb % 2, :],
                                          s["m_bc"], ALU.add), "sDVE")
            self.w(g, "sDVE", nv)
            self.inc(g.tensor_tensor(s["XN"][:, mb, :], s["tsc"][:, mb % 2, :],
                                     s["r_bc"], ALU.mult), "sPOOL")

    def ph5a(self, E):
        t, s, ps = self.t, self.s, self.ps
        q, pe, v, sc, g = E["sync"], E["tensor"], E["vector"], E["scalar"], E["gpsimd"]
        base = dict(self.cnt)
        self.inc(q.dma_start(out=s["btstage"], in_=t["bt_l"]), "sLg", 16)
        self.w(g, "sLg", self.cnt["sLg"])
        self.inc(g.tensor_copy(s["bt"], s["btstage"]), "sPOOL")
        for kb in range(NMB):
            for half in range(2):
                ins = pe.matmul(ps[0:4, 0 + half, :], s["gwt_sb"][:, kb, :],
                                s["XN"][:, kb, half * 512:(half + 1) * 512],
                                start=(kb == 0), stop=(kb == NMB - 1), skip_group_check=True)
        npe_g = self.inc(ins, "sPE")
        for kb in range(NMB):
            for half in range(2):
                ins = pe.matmul(ps[0:64, 2 + half, :], s["at_sb"][:, kb, :],
                                s["XN"][:, kb, half * 512:(half + 1) * 512],
                                start=(kb == 0), stop=(kb == NMB - 1), skip_group_check=True)
        npe_u = self.inc(ins, "sPE")
        self.w(sc, "sPE", npe_g)
        na = self.inc(sc.activation(s["gsb"], ps[0:4, 0:2, :].rearrange("p a b -> p (a b)"),
                                    AF.Identity, bias=s["gb"], scale=1.0), "sACT")
        self.w(pe, "sACT", na)
        for tc in range(8):
            ins = pe.transpose(ps[0:128, 4, tc * 4:(tc + 1) * 4],
                               s["gsb"][0:4, tc * 128:(tc + 1) * 128], s["ident32"][0:4, 0:4])
        npe_t = self.inc(ins, "sPE")
        self.w(v, "sPE", npe_t)
        nv = self.inc(v.tensor_copy(s["gtm"].rearrange("p a b -> p (a b)"), ps[:, 4, 0:32]), "sVV")
        self.w(v, "sVV", nv)
        nv = self.inc(v.tensor_reduce(s["mxg"], s["gtm"], AX.X, ALU.max), "sVV")
        self.w(v, "sVV", nv)
        for tc in range(8):
            self.inc(v.tensor_scalar(s["ohtm"][:, tc, :], s["gtm"][:, tc, :],
                                     s["mxg"][:, tc, :],
                                     None, ALU.is_ge), "sVV")
        nv = self.cnt["sVV"]
        self.w(v, "sVV", nv)
        self.w(pe, "sVV", nv)
        for tc in range(8):
            ins = pe.transpose(ps[0:4, 5, tc * 64:tc * 64 + 64].bitcast(BF16),
                               s["ohtm"][:, tc, :], s["ident_bf"])
        npe_t2 = self.inc(ins, "sPE")
        self.w(v, "sPE", npe_t2)
        nv = self.inc(v.tensor_copy(s["ohem"].rearrange("p a b -> p (a b)"),
                                    ps[0:4, 5, 0:512].bitcast(BF16)), "sVV")
        self.w(v, "sVV", nv)
        self.w(pe, "sVV", nv)
        for half in range(2):
            ins = pe.matmul(ps[0:64, 6 + half, :], s["maskv"],
                            s["ohem"].rearrange("p a b -> p (a b)")[:, half * 512:(half + 1) * 512],
                            start=True, stop=True, skip_group_check=True)
        npe_m = self.inc(ins, "sPE")
        self.w(v, "sPE", npe_m)
        nv = self.inc(v.tensor_copy(s["masksb"], ps[0:64, 6:8, :].rearrange("p a b -> p (a b)")), "sVV")
        self.w(v, "sVV", nv)
        self.w(v, "sPE", npe_u)
        self.inc(v.scalar_tensor_tensor(s["umask"], ps[0:64, 2:4, :].rearrange("p a b -> p (a b)"),
                                        1.0, s["masksb"], ALU.mult, ALU.mult), "sVV")

    def ph5b(self, E):
        self.wproj(E, "wf_l", "bfb", extra_lora=True)

    def ph5c(self, E):
        t, s, ps = self.t, self.s, self.ps
        q, pe, v, sc, g = E["sync"], E["tensor"], E["vector"], E["scalar"], E["gpsimd"]
        self.stats_to_bcast(E)
        self.bar(E)
        base = dict(self.cnt)
        lm, pms, om = [], [], []
        tpn = 0
        for mb in range(NMB):
            if mb >= 2:
                self.w(q, "sDVE", base["sDVE"] + 4 * (mb - 1))
            self.inc(q.dma_start(out=s["y_sb"][:, mb % 2, :], in_=t["ysc"][mb]), self.fsem(mb), 16)
            lm.append(self.inc(q.dma_start(out=s["inbuf"][:, mb % 2, :], in_=t["x1sc"][mb]),
                               self.fsem(mb), 16))
            self.w(v, self.fsem(mb), lm[mb])
            if mb >= 2:
                self.w(v, "sPOOL", pms[mb - 2])
                self.w(v, "sPE", base["sPE"] + 8 * (mb - 1))
            nv = self.inc(v.tensor_tensor(s["tsc"][:, mb % 2, :], s["y_sb"][:, mb % 2, :],
                                          s["m_bc"], ALU.add), "sDVE")
            self.w(g, "sDVE", nv)
            pms.append(self.inc(g.tensor_tensor(s["y_sb"][:, mb % 2, :], s["tsc"][:, mb % 2, :],
                                                s["r_bc"], ALU.mult), "sPOOL"))
            self.w(v, "sPOOL", pms[mb])
            nv = self.inc(v.tensor_tensor(s["tsc"][:, mb % 2, :], s["y_sb"][:, mb % 2, :],
                                          s["inbuf"][:, mb % 2, :], ALU.add), "sDVE")
            self.w(pe, "sDVE", nv)
            for grp in range(2):
                if tpn >= 8:
                    self.w(pe, "sDVE", base["sDVE"] + 4 * (mb - 1) + 2 + grp + 1)
                for tc4 in range(4):
                    tc = grp * 4 + tc4
                    self.inc(pe.transpose(ps[:, grp * 4 + tc4, 0:128],
                                          s["tsc"][:, mb % 2, tc * 128:(tc + 1) * 128],
                                          s["ident32"]), "sPE")
                    tpn += 1
                self.w(v, "sPE", base["sPE"] + 8 * mb + (grp + 1) * 4)
                if mb >= 2 and grp == 0:
                    self.w(v, self.xsem(mb), om[mb - 2])
                self.inc(v.tensor_copy(s["ostage"][:, mb % 2, grp * 4:(grp + 1) * 4, :],
                                       ps[:, grp * 4:grp * 4 + 4, 0:128]), "sDVE")
            nv = self.cnt["sDVE"]
            self.w(q, "sDVE", nv)
            for tc in range(8):
                ins = q.dma_start(out=t["out"][tc * 128:(tc + 1) * 128, mb * 128:(mb + 1) * 128],
                                  in_=s["ostage"][:, mb % 2, tc, :])
                self.inc(ins, self.xsem(mb), 16)
            om.append(self.cnt[self.xsem(mb)])
        self.w(q, "sX0", self.cnt["sX0"])
        self.w(q, "sX1", self.cnt["sX1"])

def build_nc(phase_limit=99, taps=()):
    nc = bass.Bass("TRN2", target_bir_lowering=False, debug=False)
    b = Builder(nc, phase_limit=phase_limit, taps=taps)
    b.emit()
    return nc, b


_BUILT = {}


def _prep_weights(inputs):
    def wlay(w):
        o, i = w.shape
        nmb = o // 128
        wt = np.ascontiguousarray(w.T)
        arr = wt.reshape(i // 128, 128, nmb, 128)
        return np.ascontiguousarray(arr.transpose(2, 1, 0, 3))

    d = {}
    d["wq_l"] = wlay(np.asarray(inputs["L0_attn_Wq"]))
    d["wk_l"] = wlay(np.asarray(inputs["L0_attn_Wk"]))
    d["wv_l"] = wlay(np.asarray(inputs["L0_attn_Wv"]))
    d["wo_l"] = wlay(np.asarray(inputs["L0_attn_Wo"]))
    d["wf_l"] = wlay(np.asarray(inputs["L0_ffn_W"]))
    d["at_l"] = np.ascontiguousarray(np.asarray(inputs["L0_ffn_A"]).T.reshape(NMB, 128, 64).transpose(1, 0, 2).reshape(128, NMB * 64))
    d["bt_l"] = np.ascontiguousarray(np.asarray(inputs["L0_ffn_B"]).T)
    d["gwt_l"] = np.ascontiguousarray(np.asarray(inputs["L0_ffn_gate_w"]).T.reshape(NMB, 128, 4).transpose(1, 0, 2).reshape(128, NMB * 4))
    d["bq_r"] = np.ascontiguousarray(np.asarray(inputs["L0_attn_bq"]).reshape(NMB, 128).T)
    d["bo_r"] = np.ascontiguousarray(np.asarray(inputs["L0_attn_bo"]).reshape(NMB, 128).T)
    d["bf_r"] = np.ascontiguousarray(np.asarray(inputs["L0_ffn_b"]).reshape(NMB, 128).T)
    d["bk_r"] = np.ascontiguousarray(np.asarray(inputs["L0_attn_bk"]).reshape(KVMB, 128).T)
    d["bv_r"] = np.ascontiguousarray(np.asarray(inputs["L0_attn_bv"]).reshape(KVMB, 128).T)
    d["gb_r"] = np.ascontiguousarray(np.asarray(inputs["L0_ffn_gate_b"]).reshape(4, 1))
    d["ident_bf"] = np.eye(128, dtype=bf16)
    d["ident32"] = np.eye(128, dtype=np.float32)
    d["ones128"] = np.ones((128, 1), dtype=bf16)
    d["onesrow32"] = np.ones((1, 128), dtype=np.float32)
    mv = np.zeros((4, 64), dtype=np.float32)
    for e, r in enumerate(RANKS):
        mv[e, :r] = 1.0
    d["maskv"] = mv.astype(bf16)
    return d


def kernel(**inputs):
    if "nc" not in _BUILT:
        nc, b = build_nc()
        _BUILT["nc"] = nc
    nc = _BUILT["nc"]
    shared = _prep_weights(inputs)
    x = np.asarray(inputs["L0_attn_x"])
    in_maps = []
    for c in range(NCORES):
        bb, sl = c // SLICES, c % SLICES
        xr = np.roll(x[bb], -sl * TOK, axis=0)
        m = dict(shared)
        m["x_fm"] = np.ascontiguousarray(xr.T)
        in_maps.append(m)
    res = run_bass_kernel_spmd(nc, in_maps, core_ids=list(range(NCORES)))
    full = np.empty((B, S, D), dtype=np.float32)
    for c in range(NCORES):
        bb, sl = c // SLICES, c % SLICES
        full[bb, sl * TOK:(sl + 1) * TOK, :] = res.results[c]["out"]
    return full


# version 09 (final)
# speedup vs baseline: 1.0938x; 1.0938x over previous
import math
import numpy as np
from contextlib import ExitStack

import concourse.bass as bass
import concourse.mybir as mybir
from concourse.bass_utils import run_bass_kernel_spmd
import ml_dtypes

F32 = mybir.dt.float32
BF16 = mybir.dt.bfloat16
AF = mybir.ActivationFunctionType
ALU = mybir.AluOpType
AX = mybir.AxisListType

B, S, D = 2, 4096, 4096
NCORES = 8
SLICES = 4
TOK = 1024
NMB = 32
KVMB = 4
HEADS = 32
KVH = 4
NKC = 32
SCALE = 1.0 / math.sqrt(128.0)
RANKS = (8, 16, 32, 64)
EPS = 1e-5
QCHUNK = 4
NCH = NMB // QCHUNK

bf16 = ml_dtypes.bfloat16

ADR = {}
_a = 0
def _put(name, nbytes, align=32):
    global _a
    _a = (_a + align - 1) // align * align
    ADR[name] = _a
    _a += nbytes
_put("ident_bf", 256); _put("ident32", 512); _put("ones128", 32)
_put("onesrow32", 512); _put("maskv", 128)
_put("bq", 128); _put("bo", 128); _put("bfb", 128); _put("bk", 64); _put("bv", 64)
_put("gb", 32)
_put("negc", 256)
_put("qb2", 256)
_put("mk2", 64)
_put("st8", 256)
_put("gwt_sb", 256)
_put("at_sb", 4096)
assert _a <= 7168, _a
ADR["B"] = 7168
ADR["C"] = ADR["B"] + 32768
ADR["D"] = ADR["C"] + 33792
ADR["E"] = ADR["D"] + 65536
ADR["F"] = ADR["E"] + 65536
assert ADR["F"] + 8063 <= 212863
ADR["wbuf32"] = ADR["B"]
ADR["wbufbf"] = ADR["B"] + 8192
ADR["y_sb"] = ADR["B"] + 12288
ADR["y2bf"] = ADR["B"] + 20480; ADR["ybf"] = ADR["B"] + 22528
ADR["bt"] = ADR["B"] + 24576
ADR["tscrow"] = ADR["F"]
ADR["m_bc"] = ADR["C"]; ADR["r_bc"] = ADR["C"] + 4096
ADR["rowA"] = ADR["C"] + 8192; ADR["rowB"] = ADR["C"] + 12288
ADR["inbuf"] = ADR["C"] + 16384
ADR["tsc"] = ADR["C"] + 24576
ADR["xfmb"] = ADR["D"]
ADR["xsq"] = ADR["D"] + 16384
ADR["sqdum"] = ADR["D"] + 20480
ADR["tsc1"] = ADR["D"] + 28672
ADR["m1b"] = ADR["D"] + 36864; ADR["r1b"] = ADR["D"] + 40960
ADR["row1A"] = ADR["D"] + 45056; ADR["row1B"] = ADR["D"] + 49152
ADR["trow1"] = ADR["D"] + 53248
ADR["umask"] = ADR["D"]; ADR["masksb"] = ADR["D"] + 2048
ADR["gsb"] = ADR["D"] + 4096; ADR["gtm"] = ADR["D"] + 8192
ADR["ohem"] = ADR["D"] + 8384; ADR["ohtm"] = ADR["D"] + 10432
ADR["mxg"] = ADR["D"] + 10496
ADR["btstage"] = ADR["D"] + 16384
ADR["ostage"] = ADR["D"]

ADR["wbufF32"] = ADR["F"]; ADR["wbufFbf"] = ADR["F"] + 4096
ADR["qsq"] = ADR["F"] + 6144
ADR["probs"] = ADR["F"]
ADR["nrm"] = ADR["F"] + 6144
_SBUF_BASE = 17024
for _k in list(ADR):
    ADR[_k] += _SBUF_BASE
assert ADR["qsq"] + 1024 <= 229247 - 129, ADR["qsq"]


class Mock:
    class MInstr:
        def then_inc(self, sem, k):
            return self
    _MI = MInstr()
    def __getattr__(self, name):
        def f(*a, **k):
            return Mock._MI
        return f


class Builder:
    def __init__(self, nc, phase_limit=99, taps=()):
        self.nc = nc
        self.phase_limit = phase_limit
        self.taps = set(taps)
        self.cnt = {}
        self.sems = {}
        self.t = {}
        self.s = {}
        self.extra_outs = []

    def declare(self, es):
        nc, t, s = self.nc, self.t, self.s
        def din(name, shape, dt=F32):
            t[name] = nc.dram_tensor(name, shape, dt, kind="ExternalInput").ap()
        din("x_fm", [D, S])
        din("wq_l", [NMB, 128, NMB, 128]); din("wo_l", [NMB, 128, NMB, 128])
        din("wf_l", [NMB, 128, NMB, 128])
        din("wk_l", [KVMB, 128, NMB, 128]); din("wv_l", [KVMB, 128, NMB, 128])
        din("at_l", [128, NMB * 64]); din("bt_l", [64, D]); din("gwt_l", [128, NMB * 4])
        din("bq_r", [128, NMB]); din("bo_r", [128, NMB]); din("bf_r", [128, NMB])
        din("bk_r", [128, KVMB]); din("bv_r", [128, KVMB]); din("gb_r", [4, 1])
        din("ident_bf", [128, 128], BF16); din("ident32", [128, 128])
        din("ones128", [128, 1], BF16); din("onesrow32", [1, 128])
        din("maskv", [4, 64], BF16)
        t["out"] = nc.dram_tensor("out", [TOK, D], F32, kind="ExternalOutput").ap()
        t["ysc"] = nc.dram_tensor("ysc", [NMB, 128, TOK], F32, kind="Internal").ap()
        t["x1sc"] = nc.dram_tensor("x1sc", [NMB, 128, TOK], F32, kind="Internal").ap()
        t["rowsc"] = nc.dram_tensor("rowsc", [2, 8, 128], F32, kind="Internal").ap()

        def at(name, shape, dt, key=None):
            h = nc.alloc_sbuf_tensor_at(name, shape, dt, offset=ADR[key or name])
            s[name] = h.ap()
        at("ident_bf", [128, 128], BF16); at("ident32", [128, 128], F32)
        at("ones128", [128, 1], BF16); at("onesrow32", [1, 128], F32)
        at("maskv", [4, 64], BF16)
        at("bq", [128, NMB], F32); at("bo", [128, NMB], F32); at("bfb", [128, NMB], F32)
        at("bk", [128, KVMB], F32); at("bv", [128, KVMB], F32); at("gb", [4, 1], F32)
        at("negc", [128, 2 * HEADS], F32)
        at("qb2", [1, 2 * HEADS], F32); at("mk2", [1, 2 * KVH], F32)
        at("st8", [128, 8, 8], F32)
        at("K", [128, KVMB, S], BF16, "B")
        at("V", [128, KVMB, S], BF16, "C")
        at("Vaug", [128, KVH, NKC, 132], BF16, "C")
        at("QA", [128, HEADS, TOK], BF16, "D")
        at("XN", [128, NMB, TOK], BF16, "E")
        at("xfmb", [128, 4, TOK], F32); at("xsq", [128, 2, TOK], BF16)
        at("sqdum", [128, D], BF16); at("tsc1", [128, 2, TOK], F32)
        at("m1b", [128, TOK], F32); at("r1b", [128, TOK], F32)
        at("row1A", [1, TOK], F32); at("row1B", [1, TOK], F32)
        at("trow1", [1, TOK], F32)
        at("wbufF32", [128, 2, QCHUNK, 128], F32); at("wbufFbf", [128, 2, QCHUNK, 128], BF16)
        at("qsq", [128, 512], BF16)
        at("probs", [128, 3, TOK], BF16)
        at("nrm", [128, 1], F32)
        at("wbuf32", [128, 4, QCHUNK, 128], F32); at("wbufbf", [128, 4, QCHUNK, 128], BF16)
        at("y_sb", [128, 2, TOK], F32); at("y2bf", [128, TOK], BF16); at("ybf", [128, TOK], BF16)
        at("bt", [64, D], BF16); at("tscrow", [1, TOK], F32)
        at("m_bc", [128, TOK], F32); at("r_bc", [128, TOK], F32)
        at("rowA", [1, TOK], F32); at("rowB", [1, TOK], F32)
        at("inbuf", [128, 2, TOK], F32); at("tsc", [128, 2, TOK], F32)
        at("umask", [64, TOK], BF16); at("masksb", [64, TOK], BF16)
        at("gsb", [4, TOK], F32); at("gtm", [128, 8, 4], F32)
        at("ohem", [4, 8, 128], BF16); at("ohtm", [128, 8, 4], BF16)
        at("mxg", [128, 8, 1], F32)
        at("ostage", [128, 2, 8, 128], F32); at("btstage", [64, D], F32)
        
        at("gwt_sb", [128, NMB, 4], BF16); at("at_sb", [128, NMB, 64], BF16)

        self.ps = nc.alloc_psum_tensor("psmaster", [128, 8, 512], F32).ap()

        for name in ["sX0", "sX1", "sF0", "sF1", "sW0", "sW1", "sW2", "sW3", "sLg",
                     "sSa", "sSb", "sSg", "sCC", "sPE", "sSC",
                     "sPV", "sEX", "sACT", "sDVE", "sVV", "sPOOL", "sBAR", "sPE2"]:
            self.sems[name] = es.enter_context(nc.semaphore(name))
            self.cnt[name] = 0

        if "ph1" in self.taps:
            self.tap("tap_xn1", [128, NMB, TOK], BF16)
            self.tap("tap_K", [128, KVMB, S], BF16)
            self.tap("tap_V", [128, KVMB, S], BF16)
            self.tap("tap_mk2", [1, 2 * KVH], F32)
        if "ph2" in self.taps:
            self.tap("tap_QA", [128, HEADS, TOK], BF16)
            self.tap("tap_negc", [128, 2 * HEADS], F32)
        if "ph25" in self.taps:
            self.tap("tap_vaug", [128, KVH, NKC, 132], BF16)
        if "ph3" in self.taps:
            self.tap("tap_att", [128, HEADS, TOK], BF16)
        if "ph4a" in self.taps:
            self.tap("tap_y", [NMB, 128, TOK], F32)
        if "ph4b" in self.taps:
            self.tap("tap_x1", [NMB, 128, TOK], F32)
            self.tap("tap_xn2", [128, NMB, TOK], BF16)
            self.tap("tap_y", [NMB, 128, TOK], F32)
        if "ph5a" in self.taps:
            self.tap("tap_um", [64, TOK], BF16)
            self.tap("tap_oh", [4, 8, 128], BF16)

    def tap(self, name, shape, dt):
        self.t[name] = self.nc.dram_tensor(name, shape, dt, kind="ExternalOutput").ap()
        self.extra_outs.append(name)

    def inc(self, instr, sem, k=1):
        instr.then_inc(self.sems[sem], k)
        self.cnt[sem] += k
        return self.cnt[sem]

    def w(self, eng, sem, val):
        eng.wait_ge(self.sems[sem], val)

    ENGINE_SEMS = {"sync": ("sX0", "sX1", "sF0", "sF1", "sW0", "sW1", "sW2", "sW3", "sLg"),
                   "tensor": ("sPE", "sSC", "sPV", "sPE2"),
                   "vector": ("sDVE", "sVV"), "scalar": ("sACT", "sEX"),
                   "gpsimd": ("sPOOL", "sSa", "sSb", "sSg", "sCC")}

    @staticmethod
    def xsem(i):
        return "sX0" if i % 2 == 0 else "sX1"

    @staticmethod
    def fsem(i):
        return "sF0" if i % 2 == 0 else "sF1"

    @staticmethod
    def wsem(i):
        return ("sW0", "sW1", "sW2", "sW3")[i % 4]

    @staticmethod
    def ssem(i):
        return "sSa" if i % 2 == 0 else "sSb"

    def bar(self, E):
        for p in ("sync", "tensor", "vector", "scalar", "gpsimd"):
            for sem in self.ENGINE_SEMS[p]:
                self.w(E[p], sem, self.cnt[sem])
            self.inc(E[p].drain(), "sBAR", 1)
        tgt = self.cnt["sBAR"]
        for p in ("sync", "tensor", "vector", "scalar", "gpsimd"):
            self.w(E[p], "sBAR", tgt)

    def emit(self):
        nc = self.nc
        es = ExitStack()
        with es:
            self.declare(es)
            block = es.enter_context(nc.Block())
            passes = ["sync", "tensor", "vector", "scalar", "gpsimd"]

            def run_pass(which, eng):
                for k in self.cnt:
                    self.cnt[k] = 0
                E = {p: (eng if p == which else Mock()) for p in passes}
                self.program(E)

            block.sync(lambda e: run_pass("sync", e))
            block.tensor(lambda e: run_pass("tensor", e))
            block.vector(lambda e: run_pass("vector", e))
            block.scalar(lambda e: run_pass("scalar", e))
            block.gpsimd(lambda e: run_pass("gpsimd", e))

    def program(self, E):
        self.ph0(E)
        self.bar(E)
        if self.phase_limit < 1:
            self.finish(E); return
        for ss in (1, 2, 3, 0):
            self.ph1_slice(E, ss)
            self.bar(E)
        self.ph1_knorm(E)
        self.bar(E)
        if "ph1" in self.taps:
            self.tapc(E, "tap_xn1", self.s["XN"])
            self.tapc(E, "tap_K", self.s["K"])
            self.tapc(E, "tap_V", self.s["V"])
            self.tapc(E, "tap_mk2", self.s["mk2"])
        if self.phase_limit < 2:
            self.finish(E); return
        self.ph2(E)
        self.bar(E)
        if "ph2" in self.taps:
            self.tapc(E, "tap_QA", self.s["QA"])
            self.tapc(E, "tap_negc", self.s["negc"])
        if self.phase_limit < 3:
            self.finish(E); return
        self.ph25(E)
        self.bar(E)
        if "ph25" in self.taps:
            self.tapc(E, "tap_vaug", self.s["Vaug"])
        if self.phase_limit < 4:
            self.finish(E); return
        self.ph3(E)
        self.bar(E)
        self.ph35(E)
        self.bar(E)
        if "ph3" in self.taps:
            self.tapc(E, "tap_att", self.s["QA"])
        if self.phase_limit < 5:
            self.finish(E); return
        self.ph4a(E)
        self.bar(E)
        if "ph4a" in self.taps:
            self.tapc(E, "tap_y", self.t["ysc"])
        if self.phase_limit < 6:
            self.finish(E); return
        self.ph4b(E)
        self.bar(E)
        if "ph4b" in self.taps:
            self.tapc(E, "tap_x1", self.t["x1sc"])
            self.tapc(E, "tap_xn2", self.s["XN"])
            self.tapc(E, "tap_y", self.t["ysc"])
        if self.phase_limit < 7:
            self.finish(E); return
        self.ph5a(E)
        self.bar(E)
        if "ph5a" in self.taps:
            self.tapc(E, "tap_um", self.s["umask"])
            self.tapc(E, "tap_oh", self.s["ohem"])
        if self.phase_limit < 8:
            self.finish(E); return
        self.ph5b(E)
        self.bar(E)
        self.ph5c(E)
        self.finish(E)

    def tapc(self, E, name, src_ap):
        n = self.inc(E["gpsimd"].dma_start(out=self.t[name], in_=src_ap), "sSg", 16)
        self.w(E["gpsimd"], "sSg", n)

    def finish(self, E):
        for sem in ("sSa", "sSb", "sSg"):
            self.w(E["gpsimd"], sem, self.cnt[sem])

    def ph0(self, E):
        t, s = self.t, self.s
        q, g = E["sync"], E["gpsimd"]
        for dst, src in [("ident_bf", "ident_bf"), ("ident32", "ident32"),
                         ("ones128", "ones128"), ("onesrow32", "onesrow32"),
                         ("maskv", "maskv"), ("gb", "gb_r"), ("bq", "bq_r"),
                         ("bo", "bo_r"), ("bfb", "bf_r"), ("bk", "bk_r"),
                         ("bv", "bv_r")]:
            self.inc(q.dma_start(out=s[dst], in_=t[src]), "sLg", 16)
        gstage = s["inbuf"].rearrange("p a f -> p (a f)")[:, 0:NMB * 4]
        astage = s["tsc"].rearrange("p a f -> p (a f)")[:, 0:NMB * 64]
        self.inc(q.dma_start(out=gstage, in_=t["gwt_l"]), "sLg", 16)
        self.inc(q.dma_start(out=astage, in_=t["at_l"]), "sLg", 16)
        self.w(g, "sLg", self.cnt["sLg"])
        self.inc(g.tensor_copy(s["gwt_sb"].rearrange("p c f -> p (c f)"), gstage), "sPOOL")
        self.inc(g.tensor_copy(s["at_sb"].rearrange("p c f -> p (c f)"), astage), "sPOOL")

    def ph1_slice(self, E, ss):
        t, s, ps = self.t, self.s, self.ps
        q, pe, v, sc, g = E["sync"], E["tensor"], E["vector"], E["scalar"], E["gpsimd"]
        base = dict(self.cnt)
        tok0 = ss * TOK

        fm, cmarks, sqm = [], [], []
        for mb in range(NMB):
            if mb >= 4:
                self.w(q, "sVV", cmarks[mb - 4])
            fm.append(self.inc(q.dma_start(out=s["xfmb"][:, mb % 4, :],
                                 in_=t["x_fm"][mb * 128:(mb + 1) * 128, tok0:tok0 + TOK]),
                               self.wsem(mb), 16))
            self.w(v, self.wsem(mb), fm[mb])
            cmarks.append(self.inc(v.tensor_copy(s["XN"][:, mb, :], s["xfmb"][:, mb % 4, :]), "sVV"))
            self.w(sc, "sVV", cmarks[mb])
            if mb >= 2:
                self.w(sc, "sPE", base["sPE"] + mb - 1)
            sqm.append(self.inc(sc.activation(s["xsq"][:, mb % 2, :], s["XN"][:, mb, :],
                                              AF.Square), "sACT"))
            self.w(pe, "sVV", cmarks[mb])
            self.w(pe, "sACT", sqm[mb])
            for half in range(2):
                pe.matmul(ps[0:1, 4 + half, :], s["ones128"],
                          s["XN"][:, mb, half * 512:(half + 1) * 512],
                          start=(mb == 0), stop=(mb == NMB - 1), skip_group_check=True)
            for half in range(2):
                ins = pe.matmul(ps[0:1, 6 + half, :], s["ones128"],
                                s["xsq"][:, mb % 2, half * 512:(half + 1) * 512],
                                start=(mb == 0), stop=(mb == NMB - 1), skip_group_check=True)
            self.inc(ins, "sPE")
        self.stats_to_bcast(E, rA="row1A", rB="row1B", trow="trow1",
                            mbc="m1b", rbc="r1b")
        nbc = self.cnt["sVV"]
        base2 = dict(self.cnt)
        apm = []
        for mb in range(NMB):
            if mb == 0:
                self.w(v, "sVV", nbc)
            if mb >= 2:
                self.w(v, "sPOOL", apm[mb - 2])
            nv = self.inc(v.tensor_tensor(s["tsc1"][:, mb % 2, :], s["XN"][:, mb, :],
                                          s["m1b"], ALU.add), "sDVE")
            self.w(g, "sDVE", nv)
            if mb == 0:
                self.w(g, "sVV", nbc)
            apm.append(self.inc(g.tensor_tensor(s["XN"][:, mb, :], s["tsc1"][:, mb % 2, :],
                                                s["r1b"], ALU.mult), "sPOOL"))
        npool_apply = self.cnt["sPOOL"]

        base3 = dict(self.cnt)
        wm, cm = [], []
        for mbkv in range(2 * KVMB):
            wsrc = t["wk_l"] if mbkv < KVMB else t["wv_l"]
            mbl = mbkv % KVMB
            pbank = (mbkv % 2) * 2
            if mbkv >= 2:
                self.w(pe, "sACT", base3["sACT"] + mbkv - 1)
            for ch in range(NCH):
                it = mbkv * NCH + ch
                if it >= 2:
                    self.w(q, "sVV", cm[it - 2])
                else:
                    self.w(q, "sVV", cmarks[NMB - 1])
                wm.append(self.inc(q.dma_start(out=s["wbufF32"][:, it % 2, :, :],
                                     in_=wsrc[mbl, :, ch * QCHUNK:(ch + 1) * QCHUNK, :]),
                                   self.wsem(it), 16))
                self.w(v, self.wsem(it), wm[it])
                if it >= 2:
                    self.w(v, "sPE2", base3["sPE2"] + 4 * (it - 1))
                cm.append(self.inc(v.tensor_copy(s["wbufFbf"][:, it % 2, :, :],
                                                 s["wbufF32"][:, it % 2, :, :]), "sVV"))
                if it == 0:
                    self.w(pe, "sPOOL", npool_apply)
                self.w(pe, "sVV", cm[it])
                for kq in range(QCHUNK):
                    kb = ch * QCHUNK + kq
                    for half in range(2):
                        ins = pe.matmul(ps[:, pbank + half, :],
                                        s["wbufFbf"][:, it % 2, kq, :],
                                        s["XN"][:, kb, half * 512:(half + 1) * 512],
                                        start=(kb == 0), stop=(kb == NMB - 1),
                                        skip_group_check=True)
                    self.inc(ins, "sPE2")
            dst = s["K"] if mbkv < KVMB else s["V"]
            bias = s["bk"] if mbkv < KVMB else s["bv"]
            self.w(sc, "sPE2", base3["sPE2"] + (mbkv + 1) * NCH * 4)
            self.inc(sc.activation(dst[:, mbkv % KVMB, tok0:tok0 + TOK],
                                   ps[:, pbank:pbank + 2, :].rearrange("p a b -> p (a b)"),
                                   AF.Identity, bias=bias[:, mbkv % KVMB:mbkv % KVMB + 1],
                                   scale=1.0), "sACT")

    def ph1_knorm(self, E):
        s, ps = self.s, self.ps
        pe, v, sc = E["tensor"], E["vector"], E["scalar"]
        base = dict(self.cnt)
        for kv in range(KVH):
            if kv >= 1:
                self.w(sc, "sPE", base["sPE"] + kv * 8)
            na = self.inc(sc.activation(s["sqdum"], s["K"][:, kv, :], AF.Square), "sACT")
            self.w(pe, "sACT", na)
            for part in range(8):
                gidx = kv * 8 + part
                if gidx >= 2:
                    self.w(pe, "sDVE", base["sDVE"] + gidx - 1)
                npe = self.inc(pe.matmul(ps[0:1, part % 2 + 4, :], s["ones128"],
                                         s["sqdum"][:, part * 512:(part + 1) * 512],
                                         start=True, stop=True, skip_group_check=True), "sPE")
                self.w(v, "sPE", npe)
                if part == 0 and kv >= 1:
                    self.w(v, "sVV", base["sVV"] + kv)
                self.inc(v.tensor_reduce(s["st8"][0:1, part, 0:1], ps[0:1, part % 2 + 4, :],
                                         AX.X, ALU.max), "sDVE")
            self.w(v, "sDVE", base["sDVE"] + (kv + 1) * 8)
            nv = self.inc(v.tensor_reduce(s["mk2"][0:1, kv:kv + 1],
                                          s["st8"][0:1, 0:8, 0:1].rearrange("p c o -> p (c o)"),
                                          AX.X, ALU.max), "sVV")
        self.w(sc, "sVV", self.cnt["sVV"])
        self.inc(sc.activation(s["mk2"][0:1, KVH:2 * KVH], s["mk2"][0:1, 0:KVH], AF.Sqrt), "sACT")

    def ph2(self, E):
        t, s, ps = self.t, self.s, self.ps
        q, pe, v, sc, g = E["sync"], E["tensor"], E["vector"], E["scalar"], E["gpsimd"]
        base = dict(self.cnt)
        wm = []
        for mb in range(NMB):
            for ch in range(NCH):
                it = mb * NCH + ch
                if it >= 2:
                    self.w(q, "sVV", base["sVV"] + it - 1)
                wm.append(self.inc(q.dma_start(out=s["wbufF32"][:, it % 2, :, :],
                                     in_=t["wq_l"][mb, :, ch * QCHUNK:(ch + 1) * QCHUNK, :]),
                                   self.wsem(it), 16))
        for it in range(NMB * NCH):
            self.w(v, self.wsem(it), wm[it])
            if it >= 2:
                self.w(v, "sPE2", base["sPE2"] + it - 1)
            self.inc(v.tensor_copy(s["wbufFbf"][:, it % 2, :, :], s["wbufF32"][:, it % 2, :, :]), "sVV")
        for mb in range(NMB):
            pbank = (mb % 2) * 2
            if mb >= 2:
                self.w(pe, "sACT", base["sACT"] + mb - 1)
            for ch in range(NCH):
                it = mb * NCH + ch
                self.w(pe, "sVV", base["sVV"] + it + 1)
                for kq in range(QCHUNK):
                    kb = ch * QCHUNK + kq
                    for half in range(2):
                        ins = pe.matmul(ps[:, pbank + half, :],
                                        s["wbufFbf"][:, it % 2, kq, :],
                                        s["XN"][:, kb, half * 512:(half + 1) * 512],
                                        start=(kb == 0), stop=(kb == NMB - 1),
                                        skip_group_check=True)
                    if kq == QCHUNK - 1:
                        self.inc(ins, "sPE2")
            self.w(sc, "sPE2", base["sPE2"] + (mb + 1) * NCH)
            self.inc(sc.activation(s["QA"][:, mb, :],
                                   ps[:, pbank:pbank + 2, :].rearrange("p a b -> p (a b)"),
                                   AF.Identity, bias=s["bq"][:, mb:mb + 1], scale=1.0), "sACT")
        base2 = dict(self.cnt)
        for h in range(HEADS):
            self.w(sc, "sACT", base["sACT"] + h + 1)
            if h >= 1:
                self.w(pe, "sDVE", base2["sDVE"] + 2 * h)
            for half in range(2):
                if not (h == 0 and half == 0):
                    self.w(sc, "sPE", base2["sPE"] + 2 * h + half)
                na = self.inc(sc.activation(s["qsq"], s["QA"][:, h, half * 512:(half + 1) * 512],
                                            AF.Square), "sACT")
                self.w(pe, "sACT", na)
                npe = self.inc(pe.matmul(ps[0:1, 4 + half, :], s["ones128"],
                                         s["qsq"],
                                         start=True, stop=True, skip_group_check=True), "sPE")
                self.w(v, "sPE", npe)
                if h + half > 0:
                    self.w(v, "sVV", base2["sVV"])
                self.inc(v.tensor_reduce(s["qb2"][0:1, 2 * h + half:2 * h + half + 1],
                                         ps[0:1, 4 + half, :], AX.X, ALU.max), "sDVE")
        self.w(sc, "sDVE", base2["sDVE"] + 2 * HEADS)
        na = self.inc(sc.activation(s["st8"][0:1, 0:8, 0:8].rearrange("p a b -> p (a b)"),
                                    s["qb2"], AF.Sqrt), "sACT")
        self.w(v, "sACT", na)
        nv = self.inc(v.tensor_copy(s["qb2"], s["st8"][0:1, 0:8, 0:8].rearrange("p a b -> p (a b)")), "sVV")
        self.w(v, "sVV", nv)
        for gi in range(KVH):
            for half in range(2):
                self.inc(v.tensor_scalar(s["qb2"][0:1, 2 * gi + half::2 * KVH],
                                         s["qb2"][0:1, 2 * gi + half::2 * KVH],
                                         s["mk2"][0:1, KVH + gi:KVH + gi + 1], SCALE,
                                         ALU.mult, ALU.mult), "sVV")
        nv = self.cnt["sVV"]
        self.w(v, "sVV", nv)
        nv = self.inc(v.tensor_scalar(s["qb2"], s["qb2"], -75.0, None, ALU.add), "sVV")
        self.w(v, "sVV", nv)
        nv = self.inc(v.tensor_scalar(s["qb2"], s["qb2"], 0.0, -1.0, ALU.max, ALU.mult), "sVV")
        self.w(pe, "sVV", nv)
        npe = self.inc(pe.matmul(ps[:, 6, 0:2 * HEADS], s["onesrow32"], s["qb2"],
                                 start=True, stop=True, skip_group_check=True), "sPE")
        self.w(v, "sPE", npe)
        self.inc(v.tensor_copy(s["negc"], ps[:, 6, 0:2 * HEADS]), "sVV")

    def ph25(self, E):
        s, ps = self.s, self.ps
        pe, v = E["tensor"], E["vector"]
        base = dict(self.cnt)
        it = 0
        for kv in range(KVH - 1, -1, -1):
            for kc in range(NKC - 1, -1, -1):
                pb = it % 4
                if it >= 4:
                    self.w(pe, "sDVE", base["sDVE"] + 3 * (it - 3))
                npe = self.inc(pe.transpose(ps[:, pb, 0:64].bitcast(BF16),
                                            s["V"][:, kv, kc * 128:(kc + 1) * 128],
                                            s["ident_bf"]), "sPE")
                self.w(v, "sPE", npe)
                self.inc(v.tensor_copy(s["Vaug"][:, kv, kc, 0:128],
                                       ps[:, pb, 0:64].bitcast(BF16)), "sDVE")
                self.inc(v.memset(s["Vaug"][:, kv, kc, 128:129], 1.0), "sDVE")
                self.inc(v.memset(s["Vaug"][:, kv, kc, 129:132], 0.0), "sDVE")
                it += 1

    def ph3(self, E):
        s, ps = self.s, self.ps
        pe, v, sc = E["tensor"], E["vector"], E["scalar"]
        base = dict(self.cnt)
        scv = [ps[:, 0:2, :], ps[:, 2:4, :]]

        def accv(j):
            return ps[:, 4 + j // 2, (j % 2) * 256:(j % 2) * 256 + 129]

        def score_emit(n):
            h, kc = divmod(n, NKC)
            if n >= 2:
                self.w(pe, "sEX", base["sEX"] + n - 1)
            for half in range(2):
                ins = pe.matmul(scv[n % 2][:, half, :],
                                s["K"][:, h % KVH, kc * 128:(kc + 1) * 128],
                                s["QA"][:, h, half * 512:(half + 1) * 512],
                                start=True, stop=True, skip_group_check=True)
            self.inc(ins, "sSC")

        def pv_emit(n):
            h, kc = divmod(n, NKC)
            self.w(pe, "sEX", base["sEX"] + n + 1)
            if kc == 0 and h >= 1:
                self.w(pe, "sDVE", base["sDVE"] + 16 * h)
            for j in range(8):
                ins = pe.matmul(accv(j), s["probs"][:, n % 3, j * 128:(j + 1) * 128],
                                s["Vaug"][:, h % KVH, kc, 0:129],
                                start=(kc == 0 and j % 2 == 0), stop=(kc == NKC - 1),
                                skip_group_check=True)
            self.inc(ins, "sPV")

        NTOT = HEADS * NKC
        score_emit(0)
        score_emit(1)
        for n in range(NTOT):
            pv_emit(n)
            if n + 2 < NTOT:
                score_emit(n + 2)
        for n in range(NTOT):
            h, kc = divmod(n, NKC)
            self.w(sc, "sSC", base["sSC"] + n + 1)
            if n >= 3:
                self.w(sc, "sPV", base["sPV"] + n - 2)
            sc.activation(s["probs"][:, n % 3, 0:512], scv[n % 2][:, 0, :],
                          AF.Exp, bias=s["negc"][:, 2 * h:2 * h + 1], scale=SCALE)
            self.inc(sc.activation(s["probs"][:, n % 3, 512:1024], scv[n % 2][:, 1, :],
                                   AF.Exp, bias=s["negc"][:, 2 * h + 1:2 * h + 2], scale=SCALE), "sEX")
        for h in range(HEADS):
            self.w(v, "sPV", base["sPV"] + (h + 1) * NKC)
            for j in range(8):
                if h + j > 0:
                    self.w(v, "sDVE", base["sDVE"] + 16 * h + 2 * j)
                nv = self.inc(v.reciprocal(s["nrm"], accv(j)[:, 128:129]), "sVV")
                self.w(v, "sVV", nv)
                self.inc(v.tensor_scalar(s["QA"][:, h, j * 128:(j + 1) * 128],
                                         accv(j)[:, 0:128], s["nrm"], None, ALU.mult),
                         "sDVE", 2)

    def ph35(self, E):
        s, ps = self.s, self.ps
        pe, v = E["tensor"], E["vector"]
        base = dict(self.cnt)
        it = 0
        for h in range(HEADS):
            for j in range(8):
                pb = it % 4
                if it >= 4:
                    self.w(pe, "sDVE", base["sDVE"] + it - 3)
                self.inc(pe.transpose(ps[:, pb, 0:64].bitcast(BF16),
                                      s["QA"][:, h, j * 128:(j + 1) * 128],
                                      s["ident_bf"]), "sPE")
                self.w(v, "sPE", base["sPE"] + it + 1)
                self.inc(v.tensor_copy(s["QA"][:, h, j * 128:(j + 1) * 128],
                                       ps[:, pb, 0:64].bitcast(BF16)), "sDVE")
                it += 1

    def wproj(self, E, wname, bias_name, extra_lora=False, rhs_name="XN"):
        t, s, ps = self.t, self.s, self.ps
        q, pe, v, sc, g = E["sync"], E["tensor"], E["vector"], E["scalar"], E["gpsimd"]
        base = dict(self.cnt)
        wsrc = t[wname]
        lora = 1 if extra_lora else 0
        wm, cm, ym, em = [], [], [], []
        for mb in range(NMB):
            pbank = (mb % 2) * 2
            if mb >= 2:
                self.w(pe, "sACT", base["sACT"] + (mb - 1))
                self.w(pe, "sDVE", base["sDVE"] + (mb - 1))
            for ch in range(NCH):
                it = mb * NCH + ch
                if it >= 4:
                    self.w(q, "sVV", cm[it - 4])
                wm.append(self.inc(q.dma_start(out=s["wbuf32"][:, it % 4, :, :],
                                     in_=wsrc[mb, :, ch * QCHUNK:(ch + 1) * QCHUNK, :]),
                                   self.wsem(it), 16))
                self.w(v, self.wsem(it), wm[it])
                if it >= 4:
                    self.w(v, "sPE2", em[it - 3])
                cm.append(self.inc(v.tensor_copy(s["wbufbf"][:, it % 4, :, :],
                                                 s["wbuf32"][:, it % 4, :, :]), "sVV"))
                self.w(pe, "sVV", cm[it])
                for kq in range(QCHUNK):
                    kb = ch * QCHUNK + kq
                    for half in range(2):
                        ne = self.inc(pe.matmul(ps[:, pbank + half, :],
                                        s["wbufbf"][:, it % 4, kq, :],
                                        s[rhs_name][:, kb, half * 512:(half + 1) * 512],
                                        start=(kb == 0),
                                        stop=(kb == NMB - 1 and not extra_lora),
                                        skip_group_check=True), "sPE2")
                em.append(ne)
            if extra_lora:
                for half in range(2):
                    ins = pe.matmul(ps[:, pbank + half, :],
                                    s["bt"][:, mb * 128:(mb + 1) * 128],
                                    s["umask"][:, half * 512:(half + 1) * 512],
                                    start=False, stop=True, skip_group_check=True)
                self.inc(ins, "sPE2")
            self.w(sc, "sPE2", self.cnt["sPE2"])
            if mb >= 2:
                self.w(sc, self.ssem(mb), ym[mb - 2])
                self.w(sc, "sDVE", base["sDVE"] + mb - 1)
                self.w(sc, "sPOOL", pm[mb - 2])
            na = self.inc(sc.activation(s["y_sb"][:, mb % 2, :],
                                        ps[:, pbank:pbank + 2, :].rearrange("p a b -> p (a b)"),
                                        AF.Identity, bias=s[bias_name][:, mb:mb + 1],
                                        scale=1.0), "sACT")
            self.w(v, "sACT", na)
            if mb >= 1:
                self.w(v, "sPE", base["sPE"] + mb)
            self.inc(v.scalar_tensor_tensor(s["y2bf"],
                                            ps[:, pbank:pbank + 2, :].rearrange("p a b -> p (a b)"),
                                            s[bias_name][:, mb:mb + 1], s["y_sb"][:, mb % 2, :],
                                            ALU.add, ALU.mult), "sDVE")
            self.w(g, "sACT", na)
            if mb >= 1:
                self.w(g, "sPE", base["sPE"] + mb)
            if mb == 0:
                pm = []
            pm.append(self.inc(g.tensor_copy(s["ybf"], s["y_sb"][:, mb % 2, :]), "sPOOL"))
            self.w(pe, "sDVE", base["sDVE"] + mb + 1)
            self.w(pe, "sPOOL", pm[mb])
            for half in range(2):
                pe.matmul(ps[0:1, 4 + half, :], s["ones128"],
                          s["ybf"][:, half * 512:(half + 1) * 512],
                          start=(mb == 0), stop=(mb == NMB - 1), skip_group_check=True)
            for half in range(2):
                ins = pe.matmul(ps[0:1, 6 + half, :], s["ones128"],
                                s["y2bf"][:, half * 512:(half + 1) * 512],
                                start=(mb == 0), stop=(mb == NMB - 1), skip_group_check=True)
            self.inc(ins, "sPE")
            self.w(g, "sACT", na)
            ym.append(self.inc(g.dma_start(out=t["ysc"][mb], in_=s["y_sb"][:, mb % 2, :]),
                               self.ssem(mb), 16))

    def stats_to_bcast(self, E, rA="rowA", rB="rowB", trow="tscrow",
                       mbc="m_bc", rbc="r_bc"):
        s, ps = self.s, self.ps
        pe, v, sc = E["tensor"], E["vector"], E["scalar"]
        self.w(v, "sPE", self.cnt["sPE"])
        nv = self.inc(v.tensor_scalar(s[rA][0:1, 0:512], ps[0:1, 4, :], 1.0 / D, None, ALU.mult), "sVV")
        nv = self.inc(v.tensor_scalar(s[rA][0:1, 512:1024], ps[0:1, 5, :], 1.0 / D, None, ALU.mult), "sVV")
        nv = self.inc(v.tensor_scalar(s[rB][0:1, 0:512], ps[0:1, 6, :], 1.0 / D, None, ALU.mult), "sVV")
        nv = self.inc(v.tensor_scalar(s[rB][0:1, 512:1024], ps[0:1, 7, :], 1.0 / D, None, ALU.mult), "sVV")
        self.w(v, "sVV", nv)
        nv = self.inc(v.tensor_tensor(s[trow], s[rA], s[rA], ALU.mult), "sVV")
        self.w(v, "sVV", nv)
        nv = self.inc(v.tensor_tensor(s[rB], s[rB], s[trow], ALU.subtract), "sVV")
        self.w(v, "sVV", nv)
        nv = self.inc(v.tensor_scalar(s[trow], s[rB], EPS, None, ALU.add), "sVV")
        self.w(sc, "sVV", nv)
        na = self.inc(sc.activation(s[rB], s[trow], AF.Sqrt), "sACT")
        self.w(v, "sACT", na)
        nv = self.inc(v.reciprocal(s[trow], s[rB]), "sVV")
        self.w(v, "sVV", nv)
        nv = self.inc(v.tensor_copy(s[rB], s[trow]), "sVV")
        nv = self.inc(v.tensor_scalar(s[rA], s[rA], -1.0, None, ALU.mult), "sVV")
        self.w(pe, "sVV", nv)
        for half in range(2):
            pe.matmul(ps[:, 0 + half, :], s["onesrow32"], s[rA][0:1, half * 512:(half + 1) * 512],
                      start=True, stop=True, skip_group_check=True)
        for half in range(2):
            ins = pe.matmul(ps[:, 2 + half, :], s["onesrow32"], s[rB][0:1, half * 512:(half + 1) * 512],
                            start=True, stop=True, skip_group_check=True)
        npe = self.inc(ins, "sPE")
        self.w(v, "sPE", npe)
        nv = self.inc(v.tensor_copy(s[mbc], ps[:, 0:2, :].rearrange("p a b -> p (a b)")), "sVV")
        self.inc(v.tensor_copy(s[rbc], ps[:, 2:4, :].rearrange("p a b -> p (a b)")), "sVV")

    def ph4a(self, E):
        self.wproj(E, "wo_l", "bo", extra_lora=False, rhs_name="QA")

    def ph4b(self, E):
        t, s, ps = self.t, self.s, self.ps
        q, pe, v, sc, g = E["sync"], E["tensor"], E["vector"], E["scalar"], E["gpsimd"]
        self.stats_to_bcast(E)
        self.bar(E)
        base = dict(self.cnt)
        lm, sm, pms = [], [], []
        for mb in range(NMB):
            if mb >= 2:
                self.w(q, "sDVE", base["sDVE"] + 4 * (mb - 1))
                self.w(q, self.ssem(mb), sm[mb - 2])
            self.inc(q.dma_start(out=s["y_sb"][:, mb % 2, :], in_=t["ysc"][mb]), self.fsem(mb), 16)
            lm.append(self.inc(q.dma_start(out=s["inbuf"][:, mb % 2, :],
                                 in_=t["x_fm"][mb * 128:(mb + 1) * 128, 0:TOK]), self.fsem(mb), 16))
            self.w(v, self.fsem(mb), lm[mb])
            if mb >= 2:
                self.w(v, self.ssem(mb), sm[mb - 2])
                self.w(v, "sPOOL", pms[mb - 2])
            nv = self.inc(v.tensor_tensor(s["tsc"][:, mb % 2, :], s["y_sb"][:, mb % 2, :],
                                          s["m_bc"], ALU.add), "sDVE")
            self.w(g, "sDVE", nv)
            pms.append(self.inc(g.tensor_tensor(s["y_sb"][:, mb % 2, :], s["tsc"][:, mb % 2, :],
                                                s["r_bc"], ALU.mult), "sPOOL"))
            self.w(v, "sPOOL", pms[mb])
            nv = self.inc(v.tensor_tensor(s["tsc"][:, mb % 2, :], s["y_sb"][:, mb % 2, :],
                                          s["inbuf"][:, mb % 2, :], ALU.add), "sDVE")
            self.w(g, "sDVE", nv)
            sm.append(self.inc(g.dma_start(out=t["x1sc"][mb], in_=s["tsc"][:, mb % 2, :]),
                               self.ssem(mb), 16))
            if mb >= 1:
                self.w(v, "sPE", base["sPE"] + mb)
            self.w(v, "sDVE", nv)
            self.inc(v.tensor_copy(s["ybf"], s["tsc"][:, mb % 2, :]), "sDVE")
            nv = self.inc(v.tensor_mul(s["y2bf"], s["tsc"][:, mb % 2, :], s["tsc"][:, mb % 2, :]), "sDVE")
            self.w(pe, "sDVE", nv)
            for half in range(2):
                pe.matmul(ps[0:1, 4 + half, :], s["ones128"],
                          s["ybf"][:, half * 512:(half + 1) * 512],
                          start=(mb == 0), stop=(mb == NMB - 1), skip_group_check=True)
            for half in range(2):
                ins = pe.matmul(ps[0:1, 6 + half, :], s["ones128"],
                                s["y2bf"][:, half * 512:(half + 1) * 512],
                                start=(mb == 0), stop=(mb == NMB - 1), skip_group_check=True)
            self.inc(ins, "sPE")
        self.bar(E)
        self.stats_to_bcast(E)
        self.bar(E)
        base2 = dict(self.cnt)
        lm2 = []
        for mb in range(NMB):
            if mb >= 2:
                self.w(q, "sDVE", base2["sDVE"] + mb - 1)
            lm2.append(self.inc(q.dma_start(out=s["inbuf"][:, mb % 2, :], in_=t["x1sc"][mb]),
                                self.fsem(mb), 16))
            self.w(v, self.fsem(mb), lm2[mb])
            if mb >= 2:
                self.w(v, "sPOOL", base2["sPOOL"] + mb - 1)
            nv = self.inc(v.tensor_tensor(s["tsc"][:, mb % 2, :], s["inbuf"][:, mb % 2, :],
                                          s["m_bc"], ALU.add), "sDVE")
            self.w(g, "sDVE", nv)
            self.inc(g.tensor_tensor(s["XN"][:, mb, :], s["tsc"][:, mb % 2, :],
                                     s["r_bc"], ALU.mult), "sPOOL")

    def ph5a(self, E):
        t, s, ps = self.t, self.s, self.ps
        q, pe, v, sc, g = E["sync"], E["tensor"], E["vector"], E["scalar"], E["gpsimd"]
        base = dict(self.cnt)
        self.inc(q.dma_start(out=s["btstage"], in_=t["bt_l"]), "sLg", 16)
        self.w(g, "sLg", self.cnt["sLg"])
        self.inc(g.tensor_copy(s["bt"], s["btstage"]), "sPOOL")
        for kb in range(NMB):
            for half in range(2):
                ins = pe.matmul(ps[0:4, 0 + half, :], s["gwt_sb"][:, kb, :],
                                s["XN"][:, kb, half * 512:(half + 1) * 512],
                                start=(kb == 0), stop=(kb == NMB - 1), skip_group_check=True)
        npe_g = self.inc(ins, "sPE")
        for kb in range(NMB):
            for half in range(2):
                ins = pe.matmul(ps[0:64, 2 + half, :], s["at_sb"][:, kb, :],
                                s["XN"][:, kb, half * 512:(half + 1) * 512],
                                start=(kb == 0), stop=(kb == NMB - 1), skip_group_check=True)
        npe_u = self.inc(ins, "sPE")
        self.w(sc, "sPE", npe_g)
        na = self.inc(sc.activation(s["gsb"], ps[0:4, 0:2, :].rearrange("p a b -> p (a b)"),
                                    AF.Identity, bias=s["gb"], scale=1.0), "sACT")
        self.w(pe, "sACT", na)
        for tc in range(8):
            ins = pe.transpose(ps[0:128, 4, tc * 4:(tc + 1) * 4],
                               s["gsb"][0:4, tc * 128:(tc + 1) * 128], s["ident32"][0:4, 0:4])
        npe_t = self.inc(ins, "sPE")
        self.w(v, "sPE", npe_t)
        nv = self.inc(v.tensor_copy(s["gtm"].rearrange("p a b -> p (a b)"), ps[:, 4, 0:32]), "sVV")
        self.w(v, "sVV", nv)
        nv = self.inc(v.tensor_reduce(s["mxg"], s["gtm"], AX.X, ALU.max), "sVV")
        self.w(v, "sVV", nv)
        for tc in range(8):
            self.inc(v.tensor_scalar(s["ohtm"][:, tc, :], s["gtm"][:, tc, :],
                                     s["mxg"][:, tc, :],
                                     None, ALU.is_ge), "sVV")
        nv = self.cnt["sVV"]
        self.w(v, "sVV", nv)
        self.w(pe, "sVV", nv)
        for tc in range(8):
            ins = pe.transpose(ps[0:4, 5, tc * 64:tc * 64 + 64].bitcast(BF16),
                               s["ohtm"][:, tc, :], s["ident_bf"])
        npe_t2 = self.inc(ins, "sPE")
        self.w(v, "sPE", npe_t2)
        nv = self.inc(v.tensor_copy(s["ohem"].rearrange("p a b -> p (a b)"),
                                    ps[0:4, 5, 0:512].bitcast(BF16)), "sVV")
        self.w(v, "sVV", nv)
        self.w(pe, "sVV", nv)
        for half in range(2):
            ins = pe.matmul(ps[0:64, 6 + half, :], s["maskv"],
                            s["ohem"].rearrange("p a b -> p (a b)")[:, half * 512:(half + 1) * 512],
                            start=True, stop=True, skip_group_check=True)
        npe_m = self.inc(ins, "sPE")
        self.w(v, "sPE", npe_m)
        nv = self.inc(v.tensor_copy(s["masksb"], ps[0:64, 6:8, :].rearrange("p a b -> p (a b)")), "sVV")
        self.w(v, "sVV", nv)
        self.w(v, "sPE", npe_u)
        self.inc(v.scalar_tensor_tensor(s["umask"], ps[0:64, 2:4, :].rearrange("p a b -> p (a b)"),
                                        1.0, s["masksb"], ALU.mult, ALU.mult), "sVV")

    def ph5b(self, E):
        self.wproj(E, "wf_l", "bfb", extra_lora=True)

    def ph5c(self, E):
        t, s, ps = self.t, self.s, self.ps
        q, pe, v, sc, g = E["sync"], E["tensor"], E["vector"], E["scalar"], E["gpsimd"]
        self.stats_to_bcast(E)
        self.bar(E)
        base = dict(self.cnt)
        lm, pms, om = [], [], []
        tpn = 0
        for mb in range(NMB):
            if mb >= 2:
                self.w(q, "sDVE", base["sDVE"] + 4 * (mb - 1))
            self.inc(q.dma_start(out=s["y_sb"][:, mb % 2, :], in_=t["ysc"][mb]), self.fsem(mb), 16)
            lm.append(self.inc(q.dma_start(out=s["inbuf"][:, mb % 2, :], in_=t["x1sc"][mb]),
                               self.fsem(mb), 16))
            self.w(v, self.fsem(mb), lm[mb])
            if mb >= 2:
                self.w(v, "sPOOL", pms[mb - 2])
                self.w(v, "sPE", base["sPE"] + 8 * (mb - 1))
            nv = self.inc(v.tensor_tensor(s["tsc"][:, mb % 2, :], s["y_sb"][:, mb % 2, :],
                                          s["m_bc"], ALU.add), "sDVE")
            self.w(g, "sDVE", nv)
            pms.append(self.inc(g.tensor_tensor(s["y_sb"][:, mb % 2, :], s["tsc"][:, mb % 2, :],
                                                s["r_bc"], ALU.mult), "sPOOL"))
            self.w(v, "sPOOL", pms[mb])
            nv = self.inc(v.tensor_tensor(s["tsc"][:, mb % 2, :], s["y_sb"][:, mb % 2, :],
                                          s["inbuf"][:, mb % 2, :], ALU.add), "sDVE")
            self.w(pe, "sDVE", nv)
            for grp in range(2):
                if tpn >= 8:
                    self.w(pe, "sDVE", base["sDVE"] + 4 * (mb - 1) + 2 + grp + 1)
                for tc4 in range(4):
                    tc = grp * 4 + tc4
                    self.inc(pe.transpose(ps[:, grp * 4 + tc4, 0:128],
                                          s["tsc"][:, mb % 2, tc * 128:(tc + 1) * 128],
                                          s["ident32"]), "sPE")
                    tpn += 1
                self.w(v, "sPE", base["sPE"] + 8 * mb + (grp + 1) * 4)
                if mb >= 2 and grp == 0:
                    self.w(v, self.ssem(mb), om[mb - 2])
                self.inc(v.tensor_copy(s["ostage"][:, mb % 2, grp * 4:(grp + 1) * 4, :],
                                       ps[:, grp * 4:grp * 4 + 4, 0:128]), "sDVE")
            nv = self.cnt["sDVE"]
            self.w(g, "sDVE", nv)
            om.append(self.inc(
                g.dma_start(out=t["out"].rearrange("(tc p) (mb c) -> p tc mb c", p=128, c=128)[:, :, mb, :],
                            in_=s["ostage"][:, mb % 2, :, :]),
                self.ssem(mb), 16))
        self.w(g, "sSa", self.cnt["sSa"])
        self.w(g, "sSb", self.cnt["sSb"])

def build_nc(phase_limit=99, taps=()):
    nc = bass.Bass("TRN2", target_bir_lowering=False, debug=False)
    b = Builder(nc, phase_limit=phase_limit, taps=taps)
    b.emit()
    return nc, b


_BUILT = {}


def _prep_weights(inputs):
    def wlay(w):
        o, i = w.shape
        nmb = o // 128
        wt = np.ascontiguousarray(w.T)
        arr = wt.reshape(i // 128, 128, nmb, 128)
        return np.ascontiguousarray(arr.transpose(2, 1, 0, 3))

    d = {}
    d["wq_l"] = wlay(np.asarray(inputs["L0_attn_Wq"]))
    d["wk_l"] = wlay(np.asarray(inputs["L0_attn_Wk"]))
    d["wv_l"] = wlay(np.asarray(inputs["L0_attn_Wv"]))
    d["wo_l"] = wlay(np.asarray(inputs["L0_attn_Wo"]))
    d["wf_l"] = wlay(np.asarray(inputs["L0_ffn_W"]))
    d["at_l"] = np.ascontiguousarray(np.asarray(inputs["L0_ffn_A"]).T.reshape(NMB, 128, 64).transpose(1, 0, 2).reshape(128, NMB * 64))
    d["bt_l"] = np.ascontiguousarray(np.asarray(inputs["L0_ffn_B"]).T)
    d["gwt_l"] = np.ascontiguousarray(np.asarray(inputs["L0_ffn_gate_w"]).T.reshape(NMB, 128, 4).transpose(1, 0, 2).reshape(128, NMB * 4))
    d["bq_r"] = np.ascontiguousarray(np.asarray(inputs["L0_attn_bq"]).reshape(NMB, 128).T)
    d["bo_r"] = np.ascontiguousarray(np.asarray(inputs["L0_attn_bo"]).reshape(NMB, 128).T)
    d["bf_r"] = np.ascontiguousarray(np.asarray(inputs["L0_ffn_b"]).reshape(NMB, 128).T)
    d["bk_r"] = np.ascontiguousarray(np.asarray(inputs["L0_attn_bk"]).reshape(KVMB, 128).T)
    d["bv_r"] = np.ascontiguousarray(np.asarray(inputs["L0_attn_bv"]).reshape(KVMB, 128).T)
    d["gb_r"] = np.ascontiguousarray(np.asarray(inputs["L0_ffn_gate_b"]).reshape(4, 1))
    d["ident_bf"] = np.eye(128, dtype=bf16)
    d["ident32"] = np.eye(128, dtype=np.float32)
    d["ones128"] = np.ones((128, 1), dtype=bf16)
    d["onesrow32"] = np.ones((1, 128), dtype=np.float32)
    mv = np.zeros((4, 64), dtype=np.float32)
    for e, r in enumerate(RANKS):
        mv[e, :r] = 1.0
    d["maskv"] = mv.astype(bf16)
    return d


def kernel(**inputs):
    if "nc" not in _BUILT:
        nc, b = build_nc()
        _BUILT["nc"] = nc
    nc = _BUILT["nc"]
    shared = _prep_weights(inputs)
    x = np.asarray(inputs["L0_attn_x"])
    in_maps = []
    for c in range(NCORES):
        bb, sl = c // SLICES, c % SLICES
        xr = np.roll(x[bb], -sl * TOK, axis=0)
        m = dict(shared)
        m["x_fm"] = np.ascontiguousarray(xr.T)
        in_maps.append(m)
    res = run_bass_kernel_spmd(nc, in_maps, core_ids=list(range(NCORES)))
    full = np.empty((B, S, D), dtype=np.float32)
    for c in range(NCORES):
        bb, sl = c // SLICES, c % SLICES
        full[bb, sl * TOK:(sl + 1) * TOK, :] = res.results[c]["out"]
    return full
